# version 06
# baseline (speedup 1.0000x reference)
import numpy as np
import ml_dtypes
import concourse.bass as bass
import concourse.mybir as mybir
from concourse.bass_utils import run_bass_kernel_spmd

F32 = mybir.dt.float32
BF16 = mybir.dt.bfloat16
I32 = mybir.dt.int32
U8 = mybir.dt.uint8
AF = mybir.ActivationFunctionType
ALU = mybir.AluOpType

NCORE = 8
T = 4096
SEQ = 2048
D = 1024
NCH = 8
DH = 64
C_CAP = 768
FUP = 2816
NEXP = 8
N_SUB = 4
DUMP_XFM = False
import os
L1_SKIP = set(os.environ.get("L1_SKIP", "").split(","))
MOE_STOP = int(os.environ.get("MOE_STOP", "99"))

BF = ml_dtypes.bfloat16


class Sem:
    def __init__(self, h):
        self.h = h
        self.n = 0

    def inc(self, instr, k=1):
        instr.then_inc(self.h, k)
        self.n += k
        return self.n


class Prog:
    def __init__(self, nc, es):
        self.nc = nc
        S = es.enter_context
        self.T = nc.tensor
        self.V = nc.vector
        self.S = nc.scalar
        self.G = nc.gpsimd
        self.Y = nc.sync
        self.qY = Sem(S(nc.semaphore("qY")))
        self.qG = Sem(S(nc.semaphore("qG")))
        self.qA = Sem(S(nc.semaphore("qA")))
        self.tS = Sem(S(nc.semaphore("tS")))
        self.vS = Sem(S(nc.semaphore("vS")))
        self.sS = Sem(S(nc.semaphore("sS")))
        self.pS = Sem(S(nc.semaphore("pS")))
        self.ccS = Sem(S(nc.semaphore("ccS")))
        self.pb = [S(nc.psum_tensor(f"pb{i}", [128, 512], F32)) for i in range(6)]
        self.pbbA = S(nc.psum_tensor("pbbA", [128, 1024], BF16))
        self.pbbB = S(nc.psum_tensor("pbbB", [128, 1024], BF16))
        self.bar_v = S(nc.sbuf_tensor("bar_v", [1, 2], F32))
        self.bar_s = S(nc.sbuf_tensor("bar_s", [1, 2], F32))
        self.bar_g = S(nc.sbuf_tensor("bar_g", [1, 2], F32))

    def dmaY(self, out, in_, **kw):
        i = self.Y.dma_start(out=out, in_=in_, **kw)
        self.qY.inc(i, 16)
        return i

    def dmaG(self, out, in_, **kw):
        i = self.G.dma_start(out=out, in_=in_, **kw)
        self.qG.inc(i, 16)
        return i

    def dmaA(self, out, in_, **kw):
        i = self.S.dma_start(out=out, in_=in_, **kw)
        self.qA.inc(i, 16)
        return i

    def barrier(self, dma=True):
        self.V.memset(self.bar_v[:], 0.0).then_inc(self.vS.h, 1)
        self.vS.n += 1
        self.S.activation(out=self.bar_s[:], in_=self.bar_s[:], func=AF.Identity).then_inc(self.sS.h, 1)
        self.sS.n += 1
        self.G.memset(self.bar_g[:], 0.0).then_inc(self.pS.h, 1)
        self.pS.n += 1
        self.T.matmul(self.pb[0][0:2, 0:2], self.bar_v[:], self.bar_v[:], start=True, stop=True).then_inc(self.tS.h, 1)
        self.tS.n += 1
        waits = [(self.tS, self.tS.n), (self.vS, self.vS.n), (self.sS, self.sS.n), (self.pS, self.pS.n)]
        if dma:
            waits += [(self.qY, self.qY.n), (self.qG, self.qG.n), (self.qA, self.qA.n)]
        for eng in (self.T, self.V, self.S, self.G, self.Y):
            for sem, v in waits:
                if v > 0:
                    eng.wait_ge(sem.h, v)


def build_program(n_sub=N_SUB, dump=DUMP_XFM):
    from contextlib import ExitStack
    nc = bass.Bass("TRN2", target_bir_lowering=False)
    es = ExitStack()
    Sx = es.enter_context

    d_x0 = nc.dram_tensor("x0_fm", [D, T], F32, kind="ExternalInput").ap()
    d_wl0 = nc.dram_tensor("w_l0qkv", [D, 3072], F32, kind="ExternalInput").ap()
    d_bl0 = nc.dram_tensor("b_l0qkv", [128, 24], F32, kind="ExternalInput").ap()
    d_wo0 = nc.dram_tensor("w_l0out", [D, D], F32, kind="ExternalInput").ap()
    d_bo0 = nc.dram_tensor("b_l0out", [128, 8], F32, kind="ExternalInput").ap()
    d_proj = nc.dram_tensor("f0_proj", [D, 16], F32, kind="ExternalInput").ap()
    d_c1 = nc.dram_tensor("f0_c1", [16, 512], F32, kind="ExternalInput").ap()
    d_c2 = nc.dram_tensor("f0_c2", [16, 512], F32, kind="ExternalInput").ap()
    d_win1 = nc.dram_tensor("f0_win1", [1024, 512], F32, kind="ExternalInput").ap()
    d_wout1t = nc.dram_tensor("f0_wout1t", [512, 4096], F32, kind="ExternalInput").ap()
    d_win2 = nc.dram_tensor("f0_win2", [4096, 512], F32, kind="ExternalInput").ap()
    d_wout2t = nc.dram_tensor("f0_wout2t", [512, D], F32, kind="ExternalInput").ap()
    d_fb1 = nc.dram_tensor("f0_b1", [128, 4], F32, kind="ExternalInput").ap()
    d_fb2 = nc.dram_tensor("f0_b2", [128, 4], F32, kind="ExternalInput").ap()
    d_wl1 = nc.dram_tensor("w_l1qkv", [D, 3072], F32, kind="ExternalInput").ap()
    d_wo1 = nc.dram_tensor("w_l1out", [D, D], F32, kind="ExternalInput").ap()
    d_wph = nc.dram_tensor("w_l1ph", [D, 2], F32, kind="ExternalInput").ap()
    d_carrier = nc.dram_tensor("l1_carrier", [16, 1], F32, kind="ExternalInput").ap()
    d_lock = nc.dram_tensor("l1_lock", [16, 1], F32, kind="ExternalInput").ap()
    d_bw = nc.dram_tensor("l1_bw", [16, 1], F32, kind="ExternalInput").ap()
    d_amask = nc.dram_tensor("l1_amask", [16, 1], U8, kind="ExternalInput").ap()
    d_carrier_r = nc.dram_tensor("l1_carrier_r", [1, 16], F32, kind="ExternalInput").ap()
    d_bw_r = nc.dram_tensor("l1_bw_r", [1, 16], F32, kind="ExternalInput").ap()
    d_wr = nc.dram_tensor("m_wr", [D, 8], F32, kind="ExternalInput").ap()
    d_br = nc.dram_tensor("m_br", [8, 1], F32, kind="ExternalInput").ap()
    d_wup = nc.dram_tensor("m_wup", [4, D, FUP], F32, kind="ExternalInput").ap()
    d_bup = nc.dram_tensor("m_bup", [4, 128, FUP // 128], F32, kind="ExternalInput").ap()
    d_wdn = nc.dram_tensor("m_wdn", [NEXP, FUP, D], F32, kind="ExternalInput").ap()
    d_bdn = nc.dram_tensor("m_bdn", [NEXP, 128, 8], F32, kind="ExternalInput").ap()
    d_ident = nc.dram_tensor("c_ident", [128, 128], BF16, kind="ExternalInput").ap()
    d_identf = nc.dram_tensor("c_identf", [128, 128], F32, kind="ExternalInput").ap()
    d_lneg = nc.dram_tensor("c_lneg", [128, 128], BF16, kind="ExternalInput").ap()
    d_e2 = nc.dram_tensor("c_e2", [2, 128], F32, kind="ExternalInput").ap()
    d_ehalf = nc.dram_tensor("c_ehalf", [16, 128], F32, kind="ExternalInput").ap()
    d_sel168 = nc.dram_tensor("c_sel168", [16, 8], F32, kind="ExternalInput").ap()

    d_y = nc.dram_tensor("y", [T, D], F32, kind="ExternalOutput").ap()
    d_xres = nc.dram_tensor("i_xres", [D, T], F32).ap()
    d_x3bf = nc.dram_tensor("i_x3bf", [T, D], BF16).ap()
    d_smi = nc.dram_tensor("i_smi", [8, T], F32)
    d_smo = nc.dram_tensor("i_smo", [64, T], F32)
    d_taui = nc.dram_tensor("i_taui", [1, 16], F32)
    d_tauo = nc.dram_tensor("i_tauo", [8, 16], F32)
    d_comp = nc.dram_tensor("i_comp", [NEXP * C_CAP + 8, 1152], BF16).ap()
    d_bisrows = nc.dram_tensor("bisrows", [32, 1], I32, kind="ExternalInput").ap()
    d_ecol = nc.dram_tensor("c_ecol", [8, 1], F32, kind="ExternalInput").ap()
    d_ydown = nc.dram_tensor("i_ydown", [C_CAP, D], F32).ap()
    if dump:
        d_dump = nc.dram_tensor("xdump", [D, T], F32, kind="ExternalOutput").ap()
        d_dbg = nc.dram_tensor("dbg", [128, 16 * 512], F32, kind="ExternalOutput").ap()

    def xres3(ch):
        return d_xres.rearrange("(c p) t -> c p t", p=128)[ch]

    warena = Sx(nc.sbuf_tensor("warena", [128, 36864], BF16))
    ln_sb = Sx(nc.sbuf_tensor("ln_sb", [128, 8 * SEQ], BF16))
    act_sb = Sx(nc.sbuf_tensor("act_sb", [128, 8 * SEQ], BF16))
    q_sb = Sx(nc.sbuf_tensor("q_sb", [128, SEQ], BF16))
    k_sb = Sx(nc.sbuf_tensor("k_sb", [128, SEQ], BF16))
    vfm_sb = Sx(nc.sbuf_tensor("vfm_sb", [128, SEQ], BF16))
    vkm_sb = Sx(nc.sbuf_tensor("vkm_sb", [128, 16 * 130], BF16))
    p_sb = Sx(nc.sbuf_tensor("p_sb", [128, 4 * 512], BF16))
    xr_sb = Sx(nc.sbuf_tensor("xr_sb", [128, 4 * 512], F32))
    sq_sb = Sx(nc.sbuf_tensor("sq_sb", [128, 512], F32))
    rb_sb = Sx(nc.sbuf_tensor("rb_sb", [128, 512], F32))
    st_sb = Sx(nc.sbuf_tensor("st_sb", [1, 8192], F32))
    sc_sb = Sx(nc.sbuf_tensor("sc_sb", [128, 64], F32))
    on_sb = Sx(nc.sbuf_tensor("on_sb", [128, 4], F32))
    onesr_sb = Sx(nc.sbuf_tensor("onesr_sb", [128, 128], F32))
    onesb_sb = Sx(nc.sbuf_tensor("onesb_sb", [128, 64], BF16))
    id_sb = Sx(nc.sbuf_tensor("id_sb", [128, 128], BF16))
    idf_sb = Sx(nc.sbuf_tensor("idf_sb", [128, 128], F32))
    lneg_sb = Sx(nc.sbuf_tensor("lneg_sb", [128, 128], BF16))
    mk_u8 = Sx(nc.sbuf_tensor("mk_u8", [128, 4], U8))
    ln_extra = Sx(nc.sbuf_tensor("ln_extra", [128, 512], F32))
    ent_sb = Sx(nc.sbuf_tensor("ent_sb", [128, 608], F32))
    i32_sb = Sx(nc.sbuf_tensor("i32_sb", [128, 268], I32))

    P = Prog(nc, es)
    sqb_sb = sq_sb[:, :].bitcast(BF16)

    def vS_(i):
        P.vS.inc(i)
        P.V.wait_ge(P.vS.h, P.vS.n)
        return i
    ones_col = on_sb[:, 0:1]
    onesrow = onesr_sb[0:1, :]

    with nc.Block():
        P.dmaY(id_sb[:], d_ident)
        P.dmaY(idf_sb[:], d_identf)
        P.dmaY(lneg_sb[:], d_lneg)
        for ch in range(NCH):
            P.dmaY(xres3(ch), d_x0.rearrange("(c p) t -> c p t", p=128)[ch])
        mz = P.V.memset(on_sb[:], 1.0)
        mz2 = P.V.memset(onesr_sb[:], 1.0)
        P.V.memset(onesb_sb[:, 0:64], 1.0)
        zc = P.V.memset(p_sb[:, :], 0.0)
        P.vS.inc(zc)
        P.Y.wait_ge(P.vS.h, P.vS.n)
        if n_sub >= 4:
            for tt in range(48):
                P.dmaY(d_comp[tt * 128:(tt + 1) * 128, :], p_sb[:, 0:1152])
            P.dmaY(d_comp[6144:6152, :], p_sb[0:8, 0:1152])
        P.barrier()

        def ln_stats(ts0, tlen=SEQ):
            for nb in range(tlen // 512):
                t0 = ts0 + nb * 512
                for ch in range(NCH):
                    buf = xr_sb[:, (ch % 2) * 512:(ch % 2) * 512 + 512]
                    P.Y.wait_ge(P.tS.h, P.tS.n)
                    P.dmaY(buf, xres3(ch)[:, t0:t0 + 512])
                    P.T.wait_ge(P.qY.h, P.qY.n)
                    P.S.wait_ge(P.qY.h, P.qY.n)
                    P.S.wait_ge(P.tS.h, P.tS.n)
                    sqv = P.S.activation(out=sqb_sb[:, 0:512], in_=buf, func=AF.Square)
                    P.sS.inc(sqv)
                    P.V.wait_ge(P.qY.h, P.qY.n)
                    P.V.wait_ge(P.tS.h, P.tS.n)
                    cbv = P.V.tensor_copy(sqb_sb[:, 512:1024], buf)
                    P.vS.inc(cbv)
                    P.T.wait_ge(P.vS.h, P.vS.n)
                    P.T.matmul(P.pb[0][0:1, :], onesb_sb[:, 0:1], sqb_sb[:, 512:1024], start=(ch == 0), stop=(ch == NCH - 1))
                    P.T.wait_ge(P.sS.h, P.sS.n)
                    mm2 = P.T.matmul(P.pb[1][0:1, :], onesb_sb[:, 0:1], sqb_sb[:, 0:512], start=(ch == 0), stop=(ch == NCH - 1))
                    P.tS.inc(mm2)
                P.V.wait_ge(P.tS.h, P.tS.n)
                o0 = nb * 512
                P.V.tensor_scalar(out=st_sb[0:1, o0:o0 + 512], in0=P.pb[0][0:1, :], scalar1=1.0 / D, scalar2=None, op0=ALU.mult)
                c2 = P.V.tensor_scalar(out=st_sb[0:1, SEQ + o0:SEQ + o0 + 512], in0=P.pb[1][0:1, :], scalar1=1.0 / D, scalar2=None, op0=ALU.mult)
                P.vS.inc(c2)
                P.T.wait_ge(P.vS.h, P.vS.n)
            m = st_sb[0:1, 0:SEQ]
            msq = st_sb[0:1, SEQ:2 * SEQ]
            nmr = st_sb[0:1, 2 * SEQ:3 * SEQ]
            m2row = st_sb[0:1, 3 * SEQ:4 * SEQ]
            vS_(P.V.tensor_tensor(out=m2row, in0=m, in1=m, op=ALU.mult))
            vS_(P.V.tensor_tensor(out=msq, in0=msq, in1=m2row, op=ALU.subtract))
            v1 = vS_(P.V.tensor_scalar(out=msq, in0=msq, scalar1=1e-5, scalar2=None, op0=ALU.add))
            P.S.wait_ge(P.vS.h, P.vS.n)
            s1 = P.S.activation(out=msq, in_=msq, func=AF.Sqrt)
            P.sS.inc(s1)
            P.V.wait_ge(P.sS.h, P.sS.n)
            vS_(P.V.reciprocal(out=msq, in_=msq))
            vS_(P.V.tensor_tensor(out=nmr, in0=m, in1=msq, op=ALU.mult))
            v3 = vS_(P.V.tensor_scalar(out=nmr, in0=nmr, scalar1=-1.0, scalar2=None, op0=ALU.mult))

        def ln_apply(ts0, t0, tn, out3, out_off=0):
            r = st_sb[0:1, SEQ:2 * SEQ]
            nmr = st_sb[0:1, 2 * SEQ:3 * SEQ]
            for nb in range(tn // 512):
                tb = t0 + nb * 512
                so = tb - ts0
                P.T.wait_ge(P.vS.h, P.vS.n)
                P.T.wait_ge(P.sS.h, P.sS.n)
                P.T.matmul(P.pb[2][:, :], onesrow, r[0:1, so:so + 512], start=True, stop=True)
                m2 = P.T.matmul(P.pb[3][:, :], onesrow, nmr[0:1, so:so + 512], start=True, stop=True)
                P.tS.inc(m2)
                for ch in range(NCH):
                    buf = xr_sb[:, (2 + (ch % 2)) * 512:(2 + (ch % 2)) * 512 + 512]
                    P.Y.wait_ge(P.vS.h, P.vS.n)
                    P.dmaY(buf, xres3(ch)[:, tb:tb + 512])
                    P.V.wait_ge(P.qY.h, P.qY.n)
                    P.V.wait_ge(P.tS.h, P.tS.n)
                    P.V.tensor_tensor(out=ln_extra[:, 0:512], in0=buf, in1=P.pb[2][:, :], op=ALU.mult)
                    vv = P.V.tensor_tensor(out=out3[:, ch, out_off + nb * 512:out_off + nb * 512 + 512], in0=ln_extra[:, 0:512], in1=P.pb[3][:, :], op=ALU.add)
                    P.vS.inc(vv)

        def qkv_pair(lidx, pr, ln3, wq):
            base = pr * 384
            for g, out_sbuf in ((0, q_sb), (1, k_sb), (2, vfm_sb)):
                for nb in range(4):
                    bank = P.pb[4 + (nb % 2)]
                    P.T.wait_ge(P.sS.h, P.sS.n)
                    mm = None
                    for ch in range(NCH):
                        mm = P.T.matmul(bank[:, :], wq[:, ch, base + g * 128: base + g * 128 + 128],
                                        ln3[:, ch, nb * 512:nb * 512 + 512], start=(ch == 0), stop=(ch == NCH - 1))
                    P.tS.inc(mm)
                    P.S.wait_ge(P.tS.h, P.tS.n)
                    bias_col = sc_sb[:, pr * 3 + g: pr * 3 + g + 1]
                    ev = P.S.activation(out=out_sbuf[:, nb * 512:nb * 512 + 512], in_=bank[:, :], func=AF.Identity, bias=bias_col, scale=1.0)
                    P.sS.inc(ev)
            P.T.wait_ge(P.sS.h, P.sS.n)

        def v_transpose(pr):
            for kt in range(16):
                P.T.transpose(P.pbbA[:, 0:64], vfm_sb[0:64, kt * 128:kt * 128 + 128], id_sb[0:64, 0:64])
                m2 = P.T.transpose(P.pbbB[:, 0:64], vfm_sb[64:128, kt * 128:kt * 128 + 128], id_sb[64:128, 64:128])
                P.tS.inc(m2)
                P.V.wait_ge(P.tS.h, P.tS.n)
                P.V.tensor_copy(vkm_sb[:, kt * 130:kt * 130 + 64], P.pbbA[:, 0:64])
                e2 = P.V.tensor_copy(vkm_sb[:, kt * 130 + 65:kt * 130 + 129], P.pbbB[:, 0:64])
                P.vS.inc(e2)
                P.T.wait_ge(P.vS.h, P.vS.n)
            vk3 = vkm_sb[:, :].rearrange("p (k c) -> p k c", c=130)
            P.V.memset(vk3[:, :, 64:65], 1.0)
            o2 = P.V.memset(vk3[:, :, 129:130], 1.0)
            P.vS.inc(o2)

        def scores_av(lidx, s, pr, qb, lng3):
            causal = lidx == 2
            ktmax = (4 * qb + 4) if causal else 16
            q0 = qb * 512
            evac_done = {}
            mm_done = {}
            moffs = {}

            def emit_scores(kt):
                diag = causal and kt >= 4 * qb
                moff = (kt - 4 * qb) * 128 if diag else 0
                bA = P.pb[0 + 2 * (kt % 2)]
                bB = P.pb[1 + 2 * (kt % 2)]
                if kt >= 2:
                    P.T.wait_ge(P.sS.h, evac_done[kt - 2])
                P.T.matmul(bA[:, moff:512], k_sb[0:64, kt * 128:kt * 128 + 128], q_sb[0:64, q0 + moff:q0 + 512],
                           start=True, stop=not diag)
                if diag:
                    P.T.matmul(bA[:, moff:moff + 128], lneg_sb[:], id_sb[:], start=False, stop=True)
                last = P.T.matmul(bB[:, moff:512], k_sb[64:128, kt * 128:kt * 128 + 128], q_sb[64:128, q0 + moff:q0 + 512],
                                  start=True, stop=not diag)
                if diag:
                    last = P.T.matmul(bB[:, moff:moff + 128], lneg_sb[:], id_sb[:], start=False, stop=True)
                mm_done[kt] = P.tS.inc(last)
                moffs[kt] = moff

            emit_scores(0)
            if ktmax > 1:
                emit_scores(1)
            for kt in range(ktmax):
                moff = moffs[kt]
                P.S.wait_ge(P.tS.h, mm_done[kt])
                bA = P.pb[0 + 2 * (kt % 2)]
                bB = P.pb[1 + 2 * (kt % 2)]
                pA = p_sb[:, (kt % 2) * 1024:(kt % 2) * 1024 + 512]
                pB = p_sb[:, (kt % 2) * 1024 + 512:(kt % 2) * 1024 + 1024]
                if causal:
                    biasA = lng3[:, s * 16 + kt, 2 * pr:2 * pr + 1]
                    biasB = lng3[:, s * 16 + kt, 2 * pr + 1:2 * pr + 2]
                else:
                    biasA = 0.0
                    biasB = 0.0
                P.S.activation(out=pA[:, moff:512], in_=bA[:, moff:512], func=AF.Exp, scale=0.125, bias=biasA)
                e2 = P.S.activation(out=pB[:, moff:512], in_=bB[:, moff:512], func=AF.Exp, scale=0.125, bias=biasB)
                evac_done[kt] = P.sS.inc(e2)
                P.T.wait_ge(P.sS.h, evac_done[kt])
                P.T.matmul(P.pb[4][0:65, moff:512], vkm_sb[:, kt * 130:kt * 130 + 65], pA[:, moff:512],
                           start=(kt == 0), stop=(kt == ktmax - 1))
                avB = P.T.matmul(P.pb[5][0:65, moff:512], vkm_sb[:, kt * 130 + 65:kt * 130 + 130], pB[:, moff:512],
                                 start=(kt == 0), stop=(kt == ktmax - 1))
                P.tS.inc(avB)
                if kt + 2 < ktmax:
                    emit_scores(kt + 2)

        def divide_store(pr, qb, ysb):
            P.S.wait_ge(P.tS.h, P.tS.n)
            rra = sqb_sb[0:1, 0:512]
            rrb = sqb_sb[0:1, 512:1024]
            l1i = P.S.activation(out=ln_extra[0:1, 0:512], in_=P.pb[4][64:65, :], func=AF.Ln)
            P.sS.inc(l1i)
            P.S.wait_ge(P.sS.h, P.sS.n)
            P.S.activation(out=rra, in_=ln_extra[0:1, 0:512], func=AF.Exp, scale=-1.0)
            l2i = P.S.activation(out=ln_extra[0:1, 0:512], in_=P.pb[5][64:65, :], func=AF.Ln)
            P.sS.inc(l2i)
            P.S.wait_ge(P.sS.h, P.sS.n)
            r2 = P.S.activation(out=rrb, in_=ln_extra[0:1, 0:512], func=AF.Exp, scale=-1.0)
            P.sS.inc(r2)
            P.T.wait_ge(P.sS.h, P.sS.n)
            P.T.matmul(P.pb[0][0:64, :], onesb_sb[0:1, 0:64], rra, start=True, stop=True, tile_position=(0, 0))
            b2 = P.T.matmul(P.pb[0][64:128, :], onesb_sb[0:1, 0:64], rrb, start=True, stop=True, tile_position=(0, 64))
            P.tS.inc(b2)
            P.V.wait_ge(P.tS.h, P.tS.n)
            c1 = P.V.tensor_copy(rb_sb[:, :], P.pb[0][:, :])
            P.vS.inc(c1)
            P.V.tensor_tensor(out=ysb[0:64, pr, qb * 512:qb * 512 + 512], in0=P.pb[4][0:64, :], in1=rb_sb[0:64, :], op=ALU.mult)
            t2 = P.V.tensor_tensor(out=ysb[64:128, pr, qb * 512:qb * 512 + 512], in0=P.pb[5][0:64, :], in1=rb_sb[64:128, :], op=ALU.mult)
            P.vS.inc(t2)
            P.T.wait_ge(P.vS.h, P.vS.n)

        def outproj_residual(lidx, s, ysb, wo):
            ts0 = s * SEQ
            for qb in range(4):
                for f in range(NCH):
                    bank = P.pb[0 + (f % 2)]
                    P.T.wait_ge(P.vS.h, P.vS.n)
                    for ch in range(NCH):
                        mm = P.T.matmul(bank[:, :], wo[:, ch, f * 128:f * 128 + 128], ysb[:, ch, qb * 512:qb * 512 + 512],
                                        start=(ch == 0), stop=(ch == NCH - 1))
                    P.tS.inc(mm)
                    xbuf = xr_sb[:, (f % 2) * 512:(f % 2) * 512 + 512]
                    P.Y.wait_ge(P.vS.h, P.vS.n)
                    P.dmaY(xbuf, xres3(f)[:, ts0 + qb * 512: ts0 + qb * 512 + 512])
                    P.V.wait_ge(P.tS.h, P.tS.n)
                    P.V.wait_ge(P.qY.h, P.qY.n)
                    bias_col = sc_sb[:, 24 + f:25 + f]
                    xout = xr_sb[:, (2 + f % 2) * 512:(2 + f % 2) * 512 + 512]
                    st1 = P.V.scalar_tensor_tensor(out=xout, in0=bank[:, :], scalar=bias_col, in1=xbuf, op0=ALU.add, op1=ALU.add)
                    P.vS.inc(st1)
                    P.Y.wait_ge(P.vS.h, P.vS.n)
                    P.dmaY(xres3(f)[:, ts0 + qb * 512: ts0 + qb * 512 + 512], xout)
                    if lidx == 2 and "tok" not in L1_SKIP:
                        asmf = [q_sb[:, :].bitcast(F32), k_sb[:, :].bitcast(F32),
                                vfm_sb[:, :].bitcast(F32), p_sb[:, 0:2048].bitcast(F32)]
                        asmb3 = ln_sb[:, 0:4096].rearrange("p (s n) -> p s n", s=4)
                        P.T.wait_ge(P.vS.h, P.vS.n)
                        for sub in range(4):
                            tpm = P.T.transpose(P.pb[2][:, 0:128], xout[:, sub * 128:sub * 128 + 128], idf_sb[:, :])
                            P.tS.inc(tpm)
                            P.V.wait_ge(P.tS.h, P.tS.n)
                            if f == 0:
                                P.V.wait_ge(P.qY.h, P.qY.n)
                                P.V.wait_ge(P.sS.h, P.sS.n)
                            cv = P.V.tensor_copy(asmf[sub][:, f * 128:f * 128 + 128], P.pb[2][:, 0:128])
                            P.vS.inc(cv)
                            P.T.wait_ge(P.vS.h, P.vS.n)
                        if f == NCH - 1:
                            rbase = (ts0 + qb * 512) // 128
                            P.Y.wait_ge(P.vS.h, P.vS.n)
                            for sub in range(4):
                                P.dmaY(d_y.rearrange("(r p) c -> r p c", p=128)[rbase + sub], asmf[sub][:, 0:1024])

        def attn_layer(lidx):
            causal = lidx == 2
            d_w = d_wl0 if lidx == 0 else d_wl1
            d_wo = d_wo0 if lidx == 0 else d_wo1
            wq = warena[:, 0:24576].rearrange("p (c n) -> p c n", c=8)
            wo = warena[:, 24576:32768].rearrange("p (c n) -> p c n", c=8)
            for ch in range(NCH):
                P.dmaG(wq[:, ch, :], d_w.rearrange("(c p) n -> c p n", p=128)[ch])
                P.dmaG(wo[:, ch, :], d_wo.rearrange("(c p) n -> c p n", p=128)[ch])
            if lidx == 0:
                P.dmaG(sc_sb[:, 0:24], d_bl0)
                P.dmaG(sc_sb[:, 24:32], d_bo0)
            else:
                z = P.V.memset(sc_sb[:, 0:32], 0.0)
                P.vS.inc(z)
            lng3 = ent_sb[:, 0:512].rearrange("p (a b) -> p a b", b=16)
            if causal and "gate" not in L1_SKIP:
                l1_trig_prep()
                P.barrier()
                for ch in range(NCH):
                    mmk = P.V.tensor_scalar(out=wo[:, ch, :], in0=wo[:, ch, :], scalar1=sc_sb[:, 40 + ch:41 + ch], scalar2=None, op0=ALU.mult)
                    P.vS.inc(mmk)
                P.barrier()
            if causal and "gate" in L1_SKIP:
                z = P.V.memset(ent_sb[:, 0:512], 0.0)
                P.vS.inc(z)
                P.barrier()
            for s in range(2):
                ts0 = s * SEQ
                with nc.named_scope(f"ln{lidx}_{s}"):
                    ln_stats(ts0)
                    P.barrier()
                    ln3 = ln_sb[:, :].rearrange("p (c n) -> p c n", c=8)
                    ln_apply(ts0, ts0, SEQ, ln3)
                    P.barrier()
                if causal and "gate" not in L1_SKIP:
                    with nc.named_scope(f"gates{s}"):
                        l1_phase_gates(s, ln3, lng3)
                        P.barrier()
                ysb = act_sb[:, :].rearrange("p (c n) -> p c n", c=8)
                with nc.named_scope(f"attn{lidx}_{s}"):
                    for pr in range(8):
                        qkv_pair(lidx, pr, ln3, wq)
                        v_transpose(pr)
                        for qb in range(4):
                            scores_av(lidx, s, pr, qb, lng3)
                            divide_store(pr, qb, ysb)
                with nc.named_scope(f"oproj{lidx}_{s}"):
                    outproj_residual(lidx, s, ysb, wo)
                    P.barrier()

        def l1_trig_prep():
            PI = 3.14159265358979
            row32 = ent_sb[0:1, 512:544]
            tmpr = ent_sb[0:1, 544:576]
            carr = ent_sb[0:1, 576:592]
            bwr = ent_sb[0:1, 592:608]
            ibwr = ent_sb[0:1, 544:560]
            cosc2 = ent_sb[0:1, 560:576]
            sinc2 = ent_sb[0:1, 576:592]
            ehalf_bf = warena[0:16, 32768:32896]
            sel_bf = warena[0:16, 32896:32904]
            m2_bf = warena[0:16, 32904:32912]
            mk_bf = warena[0:16, 32912:32913]
            P.dmaG(carr, d_carrier_r)
            P.dmaG(bwr, d_bw_r)
            P.dmaG(sc_sb[0:16, 48:49], d_lock)
            P.dmaG(sc_sb[0:16, 50:51], d_bw)
            P.dmaG(mk_u8[0:16, 0:1], d_amask)
            P.dmaG(ehalf_bf, d_ehalf)
            P.dmaG(sel_bf, d_sel168)
            P.barrier()
            vS_(P.V.tensor_copy(row32[0:1, 16:32], carr))
            vS_(P.V.tensor_scalar(out=row32[0:1, 0:16], in0=carr, scalar1=PI / 2, scalar2=None, op0=ALU.add))
            for _ in range(10):
                vS_(P.V.tensor_scalar(out=tmpr, in0=row32, scalar1=PI, scalar2=None, op0=ALU.is_gt))
                vS_(P.V.scalar_tensor_tensor(out=row32, in0=tmpr, scalar=-2 * PI, in1=row32, op0=ALU.mult, op1=ALU.add))
                vS_(P.V.tensor_scalar(out=tmpr, in0=row32, scalar1=-PI, scalar2=None, op0=ALU.is_lt))
                vS_(P.V.scalar_tensor_tensor(out=row32, in0=tmpr, scalar=2 * PI, in1=row32, op0=ALU.mult, op1=ALU.add))
            P.S.wait_ge(P.vS.h, P.vS.n)
            sn = P.S.activation(out=row32, in_=row32, func=AF.Sin)
            P.sS.inc(sn)
            P.V.wait_ge(P.sS.h, P.sS.n)
            vS_(P.V.reciprocal(out=ibwr, in_=bwr))
            vS_(P.V.tensor_tensor(out=cosc2, in0=row32[0:1, 0:16], in1=ibwr, op=ALU.mult))
            vS_(P.V.tensor_tensor(out=sinc2, in0=row32[0:1, 16:32], in1=ibwr, op=ALU.mult))
            vS_(P.V.tensor_scalar(out=sc_sb[0:16, 49:50], in0=sc_sb[0:16, 48:49], scalar1=-1.0, scalar2=1.0 + 1e-6, op0=ALU.mult, op1=ALU.add))
            vS_(P.V.reciprocal(out=sc_sb[0:16, 50:51], in_=sc_sb[0:16, 50:51]))
            vS_(P.V.tensor_scalar(out=sc_sb[0:16, 50:51], in0=sc_sb[0:16, 50:51], scalar1=-1.0, scalar2=None, op0=ALU.mult))
            vS_(P.V.tensor_copy(mk_bf, mk_u8[0:16, 0:1]))
            vS_(P.V.tensor_tensor(out=m2_bf, in0=mk_bf.broadcast_to((16, 8)), in1=sel_bf, op=ALU.mult))
            P.T.wait_ge(P.vS.h, P.vS.n)
            mk1 = P.T.matmul(P.pb[3][:, 0:8], ehalf_bf, m2_bf, start=True, stop=True)
            P.tS.inc(mk1)
            P.V.wait_ge(P.tS.h, P.tS.n)
            vS_(P.V.tensor_copy(sc_sb[:, 40:48], P.pb[3][:, 0:8]))

        def l1_phase_gates(s, ln3, lng3):
            ts0 = s * SEQ
            wph_sb = warena[:, 32920:32936].rearrange("p (c n) -> p c n", c=8)
            for ch in range(NCH):
                P.dmaG(wph_sb[:, ch, :], d_wph.rearrange("(c p) n -> c p n", p=128)[ch])
            P.barrier(dma=True)
            ph1 = st_sb[0:1, 0:SEQ]
            ph0 = st_sb[0:1, SEQ:2 * SEQ]
            n2 = st_sb[0:1, 2 * SEQ:3 * SEQ]
            rin = st_sb[0:1, 3 * SEQ:4 * SEQ]
            for b in range(4):
                mm = None
                for ch in range(NCH):
                    mm = P.T.matmul(P.pb[1][0:1, :], wph_sb[:, ch, 1:2], ln3[:, ch, b * 512:b * 512 + 512], start=(ch == 0), stop=(ch == NCH - 1))
                for ch in range(NCH):
                    mm = P.T.matmul(P.pb[2][0:1, :], wph_sb[:, ch, 0:1], ln3[:, ch, b * 512:b * 512 + 512], start=(ch == 0), stop=(ch == NCH - 1))
                P.tS.inc(mm)
                P.V.wait_ge(P.tS.h, P.tS.n)
                vS_(P.V.tensor_copy(ph1[0:1, b * 512:b * 512 + 512], P.pb[1][0:1, :]))
                vS_(P.V.tensor_copy(ph0[0:1, b * 512:b * 512 + 512], P.pb[2][0:1, :]))
                P.T.wait_ge(P.vS.h, P.vS.n)
            vS_(P.V.tensor_tensor(out=n2, in0=ph0, in1=ph0, op=ALU.mult))
            vS_(P.V.tensor_tensor(out=rin, in0=ph1, in1=ph1, op=ALU.mult))
            vS_(P.V.tensor_tensor(out=n2, in0=n2, in1=rin, op=ALU.add))
            P.S.wait_ge(P.vS.h, P.vS.n)
            sq0 = P.S.activation(out=n2, in_=n2, func=AF.Ln)
            P.sS.inc(sq0)
            P.S.wait_ge(P.sS.h, P.sS.n)
            sq = P.S.activation(out=rin, in_=n2, func=AF.Exp, scale=-0.5)
            P.sS.inc(sq)
            P.V.wait_ge(P.sS.h, P.sS.n)
            vS_(P.V.tensor_tensor(out=ph1, in0=ph1, in1=rin, op=ALU.mult))
            vS_(P.V.tensor_tensor(out=ph0, in0=ph0, in1=rin, op=ALU.mult))
            cosc2 = ent_sb[0:1, 560:576]
            sinc2 = ent_sb[0:1, 576:592]
            lngr = xr_sb[0:16, :]
            for b in range(4):
                P.T.wait_ge(P.sS.h, P.sS.n)
                P.T.wait_ge(P.vS.h, P.vS.n)
                P.T.matmul(P.pb[1][0:16, :], cosc2, ph1[0:1, b * 512:b * 512 + 512], start=True, stop=False)
                mm = P.T.matmul(P.pb[1][0:16, :], sinc2, ph0[0:1, b * 512:b * 512 + 512], start=False, stop=True)
                P.tS.inc(mm)
                P.S.wait_ge(P.tS.h, P.tS.n)
                we = P.S.activation(out=ln_extra[0:16, :], in_=P.pb[1][0:16, :], func=AF.Exp, scale=1.0, bias=sc_sb[0:16, 50:51])
                P.sS.inc(we)
                P.V.wait_ge(P.sS.h, P.sS.n)
                vS_(P.V.tensor_scalar(out=ln_extra[0:16, :], in0=ln_extra[0:16, :], scalar1=sc_sb[0:16, 48:49], scalar2=sc_sb[0:16, 49:50], op0=ALU.mult, op1=ALU.add))
                P.S.wait_ge(P.vS.h, P.vS.n)
                lg = P.S.activation(out=lngr[:, b * 512:b * 512 + 512], in_=ln_extra[0:16, :], func=AF.Ln)
                P.sS.inc(lg)
                P.T.wait_ge(P.sS.h, P.sS.n)
            for kt in range(16):
                tpm = P.T.transpose(P.pb[2][:, 0:16], lngr[:, kt * 128:kt * 128 + 128], idf_sb[0:16, 0:16])
                P.tS.inc(tpm)
                P.V.wait_ge(P.tS.h, P.tS.n)
                vS_(P.V.tensor_copy(lng3[:, s * 16 + kt, :], P.pb[2][:, 0:16]))
                P.T.wait_ge(P.vS.h, P.vS.n)

        def emit_sort16_desc(vin, bufA, bufB):
            def pat(buf, d, e, k, j):
                A = 16 // (2 * k) if k < 16 else 1
                m = k // (2 * j)
                base = d * k + e * j
                v = buf.rearrange("p (g x) -> p g x", x=16)
                flat = buf
                if k < 16:
                    r = flat.rearrange("p (g A d m e c) -> p (g A) d m e c", g=4, A=A, d=2, m=m, e=2, c=j)
                    return r[:, :, d, :, e, :]
                else:
                    r = flat.rearrange("p (g m e c) -> p g m e c", g=4, m=m, e=2, c=j)
                    return r[:, :, :, e, :]
            src, dst = vin, (bufB if vin is bufA else bufA)
            for k in [2, 4, 8, 16]:
                j = k // 2
                while j >= 1:
                    if k < 16:
                        for d, first_op in ((0, ALU.max), (1, ALU.min)):
                            lo_s, hi_s = pat(src, d, 0, k, j), pat(src, d, 1, k, j)
                            lo_d, hi_d = pat(dst, d, 0, k, j), pat(dst, d, 1, k, j)
                            sec = ALU.min if first_op == ALU.max else ALU.max
                            vS_(P.V.tensor_tensor(out=lo_d, in0=lo_s, in1=hi_s, op=first_op))
                            vS_(P.V.tensor_tensor(out=hi_d, in0=lo_s, in1=hi_s, op=sec))
                    else:
                        lo_s, hi_s = pat(src, 0, 0, k, j), pat(src, 0, 1, k, j)
                        lo_d, hi_d = pat(dst, 0, 0, k, j), pat(dst, 0, 1, k, j)
                        vS_(P.V.tensor_tensor(out=lo_d, in0=lo_s, in1=hi_s, op=ALU.max))
                        vS_(P.V.tensor_tensor(out=hi_d, in0=lo_s, in1=hi_s, op=ALU.min))
                    src, dst = dst, (bufA if dst is bufB else bufB)
                    j //= 2
            return src

        def emit_cumsum16(buf, tmp):
            src, dst = buf, tmp
            for off in (1, 2, 4, 8):
                s3 = src.rearrange("p (g c) -> p g c", c=16)
                d3 = dst.rearrange("p (g c) -> p g c", c=16)
                vS_(P.V.tensor_copy(d3[:, :, 0:off], s3[:, :, 0:off]))
                vS_(P.V.tensor_tensor(out=d3[:, :, off:16], in0=s3[:, :, off:16], in1=s3[:, :, 0:16 - off], op=ALU.add))
                src, dst = dst, src
            return src

        def ffn0_layer():
            w1 = warena[:, 0:4096].rearrange("p (c n) -> p c n", c=8)
            w1t = warena[:, 4096:20480].rearrange("p (c n) -> p c n", c=4)
            w2t = warena[:, 20480:24576].rearrange("p (c n) -> p c n", c=4)
            wpj = warena[:, 24576:24704].rearrange("p (c n) -> p c n", c=8)
            wc1 = warena[:, 24704:25216]
            wc2 = warena[:, 25216:25728]
            w2b = warena[:, 25728:26752].rearrange("p (c n) -> p c n", c=2)
            for ch in range(NCH):
                P.dmaG(w1[:, ch, :], d_win1.rearrange("(c p) n -> c p n", p=128)[ch])
                P.dmaG(wpj[:, ch, :], d_proj.rearrange("(c p) n -> c p n", p=128)[ch])
            for rc in range(4):
                P.dmaG(w1t[:, rc, :], d_wout1t.rearrange("(c p) n -> c p n", p=128)[rc])
                P.dmaG(w2t[:, rc, :], d_wout2t.rearrange("(c p) n -> c p n", p=128)[rc])
            P.dmaG(wc1[0:16, :], d_c1)
            P.dmaG(wc2[0:16, :], d_c2)
            P.dmaG(sc_sb[:, 32:36], d_fb1)
            P.dmaG(sc_sb[:, 36:40], d_fb2)
            it32 = i32_sb[:, 0:64]
            gio = P.G.iota(it32, pattern=[[0, 4], [1, 16]], base=0, channel_multiplier=0)
            P.pS.inc(gio)
            P.V.wait_ge(P.pS.h, P.pS.n)
            iota16f = ent_sb[:, 0:64]
            P.V.tensor_copy(iota16f, it32)
            rk = ent_sb[:, 64:128]
            vS_(P.V.tensor_scalar(out=rk, in0=iota16f, scalar1=1.0, scalar2=None, op0=ALU.add))
            rkv = vS_(P.V.reciprocal(out=rk, in_=rk))
            ln3f = ln_sb[:, 0:4096].rearrange("p (c n) -> p c n", n=512)
            gflat = ln_sb[:, 4096:12288].bitcast(F32)
            g3 = gflat.rearrange("p (c n) -> p c n", n=512)
            h3 = act_sb[:, :].rearrange("p (o n) -> p o n", n=512)
            af32 = act_sb[:, 0:2048].bitcast(F32)
            z0 = af32[:, 0:64]
            sA = af32[:, 64:128]
            sB = af32[:, 128:192]
            csb = af32[:, 192:256]
            cs2 = af32[:, 256:320]
            tm1 = af32[:, 320:384]
            tm2 = af32[:, 384:448]
            tau = af32[:, 448:512]
            zin = af32[:, 512:576]
            m16c = sc_sb[:, 48:52]
            rs16 = sc_sb[:, 52:56]
            zmx = sc_sb[:, 56:60]
            sup = sc_sb[:, 60:64]
            P.barrier()
            for s in range(2):
                ts0 = s * SEQ
                ln_stats(ts0)
                P.barrier()
                for blk in range(4):
                    tb = ts0 + blk * 512
                    ln_apply(ts0, tb, 512, ln3f)
                    P.barrier(dma=False)
                    dbg_on = dump and s == 0 and blk == 0

                    def dbg_dump(src_ap, off, n, rows=128):
                        if not dbg_on:
                            return
                        P.V.wait_ge(P.qY.h, P.qY.n)
                        dv = P.V.tensor_copy(rb_sb[0:rows, 0:n], src_ap)
                        P.vS.inc(dv)
                        P.Y.wait_ge(P.vS.h, P.vS.n)
                        P.dmaY(d_dbg[0:rows, off:off + n], rb_sb[0:rows, 0:n])
                    dbg_dump(ln3f[:, 0, :], 0, 512)
                    mm = None
                    for ch in range(NCH):
                        mm = P.T.matmul(P.pb[0][0:16, :], wpj[:, ch, :], ln3f[:, ch, :], start=(ch == 0), stop=(ch == NCH - 1))
                    P.tS.inc(mm)
                    P.V.wait_ge(P.tS.h, P.tS.n)
                    cva = vS_(P.V.tensor_copy(ln_extra[0:16, :], P.pb[0][0:16, :]))
                    dbg_dump(ln_extra[0:16, :], 512, 512, rows=16)
                    P.T.wait_ge(P.vS.h, P.vS.n)
                    for tt in range(4):
                        t1 = P.T.transpose(P.pb[2][:, 0:16], ln_extra[0:16, tt * 128:tt * 128 + 128], idf_sb[0:16, 0:16])
                        P.tS.inc(t1)
                        P.V.wait_ge(P.tS.h, P.tS.n)
                        cvt = vS_(P.V.tensor_copy(zin.rearrange("p (g c) -> p g c", c=16)[:, tt:tt + 1, :], P.pb[2][:, 0:16]))
                        P.T.wait_ge(P.vS.h, P.vS.n)
                    z3 = zin.rearrange("p (g c) -> p g c", c=16)
                    dbg_dump(zin, 1024, 64)
                    vS_(P.V.tensor_reduce(out=m16c, in_=z3, axis=mybir.AxisListType.X, op=ALU.add))
                    vS_(P.V.tensor_scalar(out=m16c, in0=m16c, scalar1=1.0 / 16, scalar2=None, op0=ALU.mult))
                    dbg_dump(m16c, 1408, 4)
                    dbg_dump(m16c, 1424, 4)
                    dbg_dump(zin, 1472, 64)
                    if dbg_on:
                        vS_(P.V.tensor_reduce(out=tm2[:, 8:12], in_=z3, axis=mybir.AxisListType.X, op=ALU.add))
                        vS_(P.V.tensor_scalar(out=tm2[:, 8:12], in0=tm2[:, 8:12], scalar1=1.0 / 16, scalar2=None, op0=ALU.mult))
                        dbg_dump(tm2[:, 8:12], 1432, 4)
                    vS_(P.V.tensor_tensor(out=tm1, in0=zin, in1=zin, op=ALU.mult))
                    vS_(P.V.tensor_reduce(out=rs16, in_=tm1.rearrange("p (g c) -> p g c", c=16), axis=mybir.AxisListType.X, op=ALU.add))
                    vS_(P.V.tensor_scalar(out=rs16, in0=rs16, scalar1=1.0 / 16, scalar2=None, op0=ALU.mult))
                    vS_(P.V.tensor_tensor(out=tm2[:, 0:4], in0=m16c, in1=m16c, op=ALU.mult))
                    vS_(P.V.tensor_tensor(out=rs16, in0=rs16, in1=tm2[:, 0:4], op=ALU.subtract))
                    v1 = vS_(P.V.tensor_scalar(out=rs16, in0=rs16, scalar1=1e-5, scalar2=None, op0=ALU.add))
                    P.S.wait_ge(P.vS.h, P.vS.n)
                    s1 = P.S.activation(out=rs16, in_=rs16, func=AF.Sqrt)
                    P.sS.inc(s1)
                    P.V.wait_ge(P.sS.h, P.sS.n)
                    vS_(P.V.reciprocal(out=rs16, in_=rs16))
                    dbg_dump(rs16, 1416, 4)
                    m3 = m16c.rearrange("p (g o) -> p g o", o=1).broadcast_to((128, 4, 16))
                    r3 = rs16.rearrange("p (g o) -> p g o", o=1).broadcast_to((128, 4, 16))
                    vS_(P.V.tensor_tensor(out=z3, in0=z3, in1=m3, op=ALU.subtract))
                    vS_(P.V.tensor_tensor(out=z3, in0=z3, in1=r3, op=ALU.mult))
                    vS_(P.V.tensor_scalar(out=zin, in0=zin, scalar1=0.5, scalar2=None, op0=ALU.mult))
                    vS_(P.V.tensor_reduce(out=zmx, in_=z3, axis=mybir.AxisListType.X, op=ALU.max))
                    zx3 = zmx.rearrange("p (g o) -> p g o", o=1).broadcast_to((128, 4, 16))
                    vS_(P.V.tensor_tensor(out=z0.rearrange("p (g c) -> p g c", c=16), in0=z3, in1=zx3, op=ALU.subtract))
                    dbg_dump(zin, 1088, 64)
                    dbg_dump(z0, 1152, 64)
                    vS_(P.V.tensor_copy(sA, z0))
                    sorted_buf = emit_sort16_desc(sA, sA, sB)
                    vS_(P.V.tensor_copy(csb, sorted_buf))
                    csr = emit_cumsum16(csb, tm2)
                    vS_(P.V.tensor_tensor(out=cs2, in0=sorted_buf, in1=sorted_buf, op=ALU.mult))
                    cs2r = emit_cumsum16(cs2, tm2)
                    vS_(P.V.tensor_tensor(out=csr, in0=csr, in1=rk, op=ALU.mult))
                    vS_(P.V.tensor_tensor(out=cs2r, in0=cs2r, in1=rk, op=ALU.mult))
                    csb2 = csr
                    vS_(P.V.tensor_tensor(out=tm1, in0=csb2, in1=csb2, op=ALU.mult))
                    vS_(P.V.tensor_tensor(out=tm1, in0=cs2r, in1=tm1, op=ALU.subtract))
                    vS_(P.V.tensor_scalar(out=tm2, in0=iota16f, scalar1=1.0, scalar2=None, op0=ALU.add))
                    vS_(P.V.tensor_tensor(out=tm1, in0=tm1, in1=tm2, op=ALU.mult))
                    vS_(P.V.tensor_scalar(out=tm1, in0=tm1, scalar1=-1.0, scalar2=1.0, op0=ALU.mult, op1=ALU.add))
                    vS_(P.V.tensor_tensor(out=tm1, in0=tm1, in1=rk, op=ALU.mult))
                    v2 = vS_(P.V.tensor_scalar(out=tm1, in0=tm1, scalar1=0.0, scalar2=None, op0=ALU.max))
                    P.S.wait_ge(P.vS.h, P.vS.n)
                    s2 = P.S.activation(out=tm1, in_=tm1, func=AF.Sqrt)
                    P.sS.inc(s2)
                    P.V.wait_ge(P.sS.h, P.sS.n)
                    vS_(P.V.tensor_tensor(out=tau, in0=csb2, in1=tm1, op=ALU.subtract))
                    vS_(P.V.tensor_tensor(out=tm1, in0=tau, in1=sorted_buf, op=ALU.is_le))
                    vS_(P.V.tensor_reduce(out=sup, in_=tm1.rearrange("p (g c) -> p g c", c=16), axis=mybir.AxisListType.X, op=ALU.add))
                    vS_(P.V.tensor_scalar(out=sup, in0=sup, scalar1=-1.0, scalar2=None, op0=ALU.add))
                    s3b = sup.rearrange("p (g o) -> p g o", o=1).broadcast_to((128, 4, 16))
                    vS_(P.V.tensor_tensor(out=tm1, in0=iota16f.rearrange("p (g c) -> p g c", c=16), in1=s3b, op=ALU.is_equal))
                    vS_(P.V.tensor_tensor(out=tm1, in0=tau, in1=tm1, op=ALU.mult))
                    vS_(P.V.tensor_reduce(out=zmx, in_=tm1.rearrange("p (g c) -> p g c", c=16), axis=mybir.AxisListType.X, op=ALU.add))
                    tx3 = zmx.rearrange("p (g o) -> p g o", o=1).broadcast_to((128, 4, 16))
                    vS_(P.V.tensor_tensor(out=z0.rearrange("p (g c) -> p g c", c=16), in0=z0.rearrange("p (g c) -> p g c", c=16), in1=tx3, op=ALU.subtract))
                    v4 = vS_(P.V.tensor_scalar(out=z0, in0=z0, scalar1=0.0, scalar2=None, op0=ALU.max))
                    P.S.wait_ge(P.vS.h, P.vS.n)
                    s4 = P.S.activation(out=z0, in_=z0, func=AF.Square)
                    P.sS.inc(s4)
                    if dbg_on:
                        P.V.wait_ge(P.sS.h, P.sS.n)
                    dbg_dump(z0, 1216, 64)
                    dbg_dump(sorted_buf, 1280, 64)
                    dbg_dump(tau, 1344, 64)
                    a_fm = vkm_sb[0:16, 0:512]
                    P.T.wait_ge(P.sS.h, P.sS.n)
                    for tt in range(4):
                        t2 = P.T.transpose(P.pb[2][0:16, 0:128], z0[:, tt * 16:tt * 16 + 16], idf_sb[:, :])
                        P.tS.inc(t2)
                        P.V.wait_ge(P.tS.h, P.tS.n)
                        cvb = vS_(P.V.tensor_copy(a_fm[:, tt * 128:tt * 128 + 128], P.pb[2][0:16, 0:128]))
                        P.T.wait_ge(P.vS.h, P.vS.n)
                    for half, wc in ((0, wc1), (1, wc2)):
                        for rc in range(4):
                            g1m = P.T.matmul(P.pb[3][:, :], wc[0:16, rc * 128:rc * 128 + 128], a_fm, start=True, stop=True)
                            P.tS.inc(g1m)
                            P.V.wait_ge(P.tS.h, P.tS.n)
                            gv = vS_(P.V.tensor_copy(g3[:, half * 4 + rc, :], P.pb[3][:, :]))
                            P.T.wait_ge(P.vS.h, P.vS.n)
                    dbg_dump(g3[:, 0, :], 1536, 512)
                    dbg_dump(a_fm, 2048, 512, rows=16)
                    z1g = q_sb[:, :].rearrange("p (c n) -> p c n", n=512)
                    for rc in range(4):
                        bank = P.pb[0 + (rc % 2)]
                        P.T.wait_ge(P.vS.h, P.vS.n)
                        mm = None
                        for ch in range(NCH):
                            mm = P.T.matmul(bank[:, :], w1[:, ch, rc * 128:rc * 128 + 128], ln3f[:, ch, :], start=(ch == 0), stop=(ch == NCH - 1))
                        P.tS.inc(mm)
                        P.V.wait_ge(P.tS.h, P.tS.n)
                        sv = P.V.scalar_tensor_tensor(out=z1g[:, rc, :], in0=bank[:, :], scalar=sc_sb[:, 32 + rc:33 + rc], in1=g3[:, rc, :], op0=ALU.add, op1=ALU.mult)
                        P.vS.inc(sv)
                    dbg_dump(z1g[:, 0, :], 2560, 512)
                    P.T.wait_ge(P.vS.h, P.vS.n)
                    for oc in range(32):
                        bank = P.pb[0 + (oc % 2)]
                        P.T.wait_ge(P.sS.h, P.sS.n)
                        mm = None
                        for rc in range(4):
                            mm = P.T.matmul(bank[:, :], w1t[:, rc, oc * 128:oc * 128 + 128], z1g[:, rc, :], start=(rc == 0), stop=(rc == 3))
                        P.tS.inc(mm)
                        P.S.wait_ge(P.tS.h, P.tS.n)
                        ge = P.S.activation(out=h3[:, oc, :], in_=bank[:, :], func=AF.Gelu)
                        P.sS.inc(ge)
                    if dbg_on:
                        P.V.wait_ge(P.sS.h, P.sS.n)
                    dbg_dump(h3[:, 0, :], 3072, 512)
                    z2g = k_sb[:, :].rearrange("p (c n) -> p c n", n=512)
                    P.T.wait_ge(P.sS.h, P.sS.n)
                    loaded = []
                    mmoc = {}
                    for oc in range(2):
                        P.dmaG(w2b[:, oc % 2, :], d_win2.rearrange("(c p) n -> c p n", p=128)[oc])
                        loaded.append(P.qG.n)
                    for oc in range(32):
                        wslot = w2b[:, oc % 2, :]
                        P.T.wait_ge(P.qG.h, loaded[oc])
                        mm = None
                        for rc in range(4):
                            mm = P.T.matmul(P.pb[2 + rc][:, :], wslot[:, rc * 128:rc * 128 + 128], h3[:, oc, :], start=(oc == 0), stop=(oc == 31))
                        mmoc[oc] = P.tS.inc(mm)
                        if oc + 2 < 32:
                            P.G.wait_ge(P.tS.h, mmoc[oc])
                            P.dmaG(w2b[:, oc % 2, :], d_win2.rearrange("(c p) n -> c p n", p=128)[oc + 2])
                            loaded.append(P.qG.n)
                    P.V.wait_ge(P.tS.h, P.tS.n)
                    for rc in range(4):
                        sv = P.V.scalar_tensor_tensor(out=z2g[:, rc, :], in0=P.pb[2 + rc][:, :], scalar=sc_sb[:, 36 + rc:37 + rc], in1=g3[:, 4 + rc, :], op0=ALU.add, op1=ALU.mult)
                        P.vS.inc(sv)
                    dbg_dump(z2g[:, 0, :], 3584, 512)
                    P.T.wait_ge(P.vS.h, P.vS.n)
                    for f in range(NCH):
                        bank = P.pb[0 + (f % 2)]
                        P.T.wait_ge(P.vS.h, P.vS.n)
                        mm = None
                        for rc in range(4):
                            mm = P.T.matmul(bank[:, :], w2t[:, rc, f * 128:f * 128 + 128], z2g[:, rc, :], start=(rc == 0), stop=(rc == 3))
                        P.tS.inc(mm)
                        xbuf = xr_sb[:, (f % 2) * 512:(f % 2) * 512 + 512]
                        P.Y.wait_ge(P.vS.h, P.vS.n)
                        P.dmaY(xbuf, xres3(f)[:, tb:tb + 512])
                        P.V.wait_ge(P.tS.h, P.tS.n)
                        P.V.wait_ge(P.qY.h, P.qY.n)
                        xout = xr_sb[:, (2 + f % 2) * 512:(2 + f % 2) * 512 + 512]
                        st1 = P.V.scalar_tensor_tensor(out=xout, in0=bank[:, :], scalar=0.0, in1=xbuf, op0=ALU.add, op1=ALU.add)
                        P.vS.inc(st1)
                        P.Y.wait_ge(P.vS.h, P.vS.n)
                        P.dmaY(xres3(f)[:, tb:tb + 512], xout)
                    P.barrier(dma=False)
                P.barrier()


        def moe_layer():
            TRASH = NEXP * C_CAP + 2
            actf = act_sb[:, :].bitcast(F32)
            lnf = ln_sb[:, :].bitcast(F32)
            ln3f = ln_sb[:, 0:4096].rearrange("p (c n) -> p c n", n=512)
            sm8 = actf[0:8, 0:4096]
            mk8 = actf[0:8, 4096:8192]
            pos8 = lnf[0:8, 2048:6144]
            sel8 = lnf[0:8, 6144:8192]
            wrb = vkm_sb[:, 1024:1152].bitcast(F32).rearrange("p (c n) -> p c n", c=8)
            ln4f3 = lnf[:, 4096:8192].rearrange("p (c n) -> p c n", n=512)
            dest_i = i32_sb[:, 0:256].rearrange("p (t e) -> p t e", e=8)
            x3s = warena[:, 0:36864].rearrange("p (t c) -> p t c", c=1152)
            for ch in range(NCH):
                P.dmaY(wrb[:, ch, :], d_wr.rearrange("(c p) n -> c p n", p=128)[ch])
            P.dmaG(sq_sb[0:8, 136:137] if False else sc_sb[0:8, 62:63], d_br)
            P.dmaG(i32_sb[0:32, 256:257], d_bisrows)
            P.dmaG(sc_sb[0:8, 63:64], d_ecol)
            for s in range(2):
              with nc.named_scope(f"router{s}"):
                ts0 = s * SEQ
                ln_stats(ts0)
                P.barrier()
                for blk in range(4):
                    tb = ts0 + blk * 512
                    ln_apply(ts0, tb, 512, ln4f3)
                    P.barrier()
                    mm = None
                    for ch in range(NCH):
                        mm = P.T.matmul(P.pb[0][0:8, :], wrb[:, ch, :], ln4f3[:, ch, :], start=(ch == 0), stop=(ch == NCH - 1))
                    P.tS.inc(mm)
                    P.S.wait_ge(P.tS.h, P.tS.n)
                    ex = P.S.activation(out=sm8[:, tb:tb + 512], in_=P.pb[0][0:8, :], func=AF.Exp, scale=1.0, bias=sc_sb[0:8, 62:63])
                    P.sS.inc(ex)
                    P.T.wait_ge(P.sS.h, P.sS.n)
                    asmb3r = ln_sb[:, 0:4096].rearrange("p (s n) -> p s n", s=4)
                    for f in range(NCH):
                        for sub in range(4):
                            tpl = P.T.transpose(P.pb[2 + (sub % 2)][:, 0:128], ln4f3[:, f, sub * 128:sub * 128 + 128], idf_sb[:, :])
                            P.tS.inc(tpl)
                            P.S.wait_ge(P.tS.h, P.tS.n)
                            if f == 0:
                                P.S.wait_ge(P.qY.h, P.qY.n)
                            cbl = P.S.activation(out=asmb3r[:, sub, f * 128:f * 128 + 128], in_=P.pb[2 + (sub % 2)][:, 0:128], func=AF.Identity)
                            P.sS.inc(cbl)
                            P.T.wait_ge(P.sS.h, P.sS.n)
                    P.Y.wait_ge(P.sS.h, P.sS.n)
                    for sub in range(4):
                        P.dmaY(d_x3bf.rearrange("(r p) c -> r p c", p=128)[(tb // 128) + sub], asmb3r[:, sub, :])
                    P.barrier()
            P.barrier()
            for blk in range(8):
                t0 = blk * 512
                m1 = P.T.matmul(P.pb[1][0:1, :], on_sb[0:8, 0:1], sm8[:, t0:t0 + 512], start=True, stop=True)
                P.tS.inc(m1)
                P.V.wait_ge(P.tS.h, P.tS.n)
                vS_(P.V.reciprocal(out=st_sb[0:1, t0:t0 + 512], in_=P.pb[1][0:1, :]))
                P.T.wait_ge(P.vS.h, P.vS.n)
                m2 = P.T.matmul(P.pb[2][0:8, :], onesr_sb[0:1, 0:8], st_sb[0:1, t0:t0 + 512], start=True, stop=True)
                P.tS.inc(m2)
                P.V.wait_ge(P.tS.h, P.tS.n)
                vS_(P.V.tensor_tensor(out=sm8[:, t0:t0 + 512], in0=sm8[:, t0:t0 + 512], in1=P.pb[2][0:8, :], op=ALU.mult))
                P.T.wait_ge(P.vS.h, P.vS.n)
            if MOE_STOP <= 1:
                return
            P.Y.wait_ge(P.vS.h, P.vS.n)
            P.dmaY(d_smi.ap(), sm8)
            P.G.wait_ge(P.qY.h, P.qY.n)
            cc1 = P.G.collective_compute(
                "AllGather", ALU.bypass, replica_groups=[list(range(NCORE))],
                ins=[d_smi.ap().opt()], outs=[d_smo.ap().opt()])
            P.ccS.inc(cc1, 1)
            P.G.wait_ge(P.ccS.h, P.ccS.n)
            bis = xr_sb[0:32, 0:1024]
            gi = P.G.indirect_dma_start(
                out=bis, out_offset=None,
                in_=d_smo.ap().rearrange("a (q u) -> (a q) u", u=1024),
                in_offset=bass.IndirectOffsetOnAxis(ap=i32_sb[0:32, 256:257], axis=0))
            P.qG.inc(gi, 16)
            P.barrier()
            if MOE_STOP <= 2:
                return
            bis_scope = nc.enter_named_scope("bisect", False)
            mskb = xr_sb[32:64, 0:1024]
            cells = sq_sb[0:1, 128:136]
            lo_c = sq_sb[0:1, 128:129]
            hi_c = sq_sb[0:1, 129:130]
            mid_c = sq_sb[0:1, 130:131]
            ge_c = sq_sb[0:1, 131:132]
            tot_c = sq_sb[0:1, 132:133]
            vS_(P.V.memset(lo_c, 0.0))
            vS_(P.V.memset(hi_c, 1.0))
            vS_(P.V.memset(mid_c, 0.5))
            tau_col = sq_sb[0:32, 32:33]
            for it in range(36):
                vS_(P.V.tensor_scalar(out=sq_sb[0:1, 0:32], in0=onesr_sb[0:1, 0:32], scalar1=mid_c, scalar2=None, op0=ALU.mult))
                vS_(P.V.transpose(out=sq_sb[0:32, 32:64], in_=sq_sb[0:32, 0:32]))
                vS_(P.V.tensor_scalar(out=mskb, in0=bis, scalar1=tau_col, scalar2=0.0, op0=ALU.is_ge, op1=ALU.add, accum_out=sq_sb[0:32, 64:65]))
                vS_(P.V.transpose(out=sq_sb[0:32, 96:128], in_=sq_sb[0:32, 64:96]))
                vS_(P.V.tensor_reduce(out=tot_c, in_=sq_sb[0:1, 96:128], axis=mybir.AxisListType.X, op=ALU.add))
                vS_(P.V.tensor_scalar(out=ge_c, in0=tot_c, scalar1=4095.5, scalar2=None, op0=ALU.is_ge))
                vS_(P.V.select(out=lo_c, mask=ge_c.bitcast(I32), on_true=mid_c, on_false=lo_c))
                vS_(P.V.select(out=hi_c, mask=ge_c.bitcast(I32), on_true=hi_c, on_false=mid_c))
                vS_(P.V.tensor_tensor(out=mid_c, in0=lo_c, in1=hi_c, op=ALU.add))
                vS_(P.V.tensor_scalar(out=mid_c, in0=mid_c, scalar1=0.5, scalar2=None, op0=ALU.mult))
            P.Y.wait_ge(P.vS.h, P.vS.n)
            if MOE_STOP <= 3:
                return
            P.dmaY(d_taui.ap(), sq_sb[0:1, 128:144])
            P.G.wait_ge(P.qY.h, P.qY.n)
            cc2 = P.G.collective_compute(
                "AllGather", ALU.bypass, replica_groups=[list(range(NCORE))],
                ins=[d_taui.ap().opt()], outs=[d_tauo.ap().opt()])
            P.ccS.inc(cc2, 1)
            P.G.wait_ge(P.ccS.h, P.ccS.n)
            gtau = P.G.dma_start(out=sq_sb[0:8, 136:137], in_=d_tauo.ap()[:, 0:1], allow_slow_non_contiguous=True)
            P.qG.inc(gtau, 16)
            P.barrier()
            nc.leave_named_scope("bisect", bis_scope[0], False)
            if MOE_STOP <= 4:
                return
            vS_(P.V.tensor_scalar(out=mk8, in0=sm8, scalar1=sq_sb[0:8, 136:137], scalar2=None, op0=ALU.is_ge))
            vS_(P.V.tensor_tensor_scan(out=pos8, data0=on_sb[0:8, 0:1].broadcast_to((8, 4096)), data1=mk8, initial=0.0, op0=ALU.mult, op1=ALU.add))
            for hf in range(2):
                hs = slice(hf * 2048, hf * 2048 + 2048)
                vS_(P.V.tensor_scalar(out=sel8, in0=pos8[:, hs], scalar1=float(C_CAP) + 0.5, scalar2=None, op0=ALU.is_le))
                vS_(P.V.tensor_tensor(out=mk8[:, hs], in0=mk8[:, hs], in1=sel8, op=ALU.mult))
                vS_(P.V.tensor_scalar(out=pos8[:, hs], in0=pos8[:, hs], scalar1=sc_sb[0:8, 63:64], scalar2=None, op0=ALU.add))
                vS_(P.V.tensor_scalar(out=pos8[:, hs], in0=pos8[:, hs], scalar1=-float(TRASH), scalar2=None, op0=ALU.add))
                vS_(P.V.tensor_tensor(out=pos8[:, hs], in0=pos8[:, hs], in1=mk8[:, hs], op=ALU.mult))
                vS_(P.V.tensor_scalar(out=pos8[:, hs], in0=pos8[:, hs], scalar1=float(TRASH), scalar2=None, op0=ALU.add))
            if dump:
                def mdump(src_ap, off, n, rows=128):
                    P.V.wait_ge(P.qY.h, P.qY.n)
                    dv = P.V.tensor_copy(rb_sb[0:rows, 0:n], src_ap)
                    P.vS.inc(dv)
                    P.Y.wait_ge(P.vS.h, P.vS.n)
                    P.dmaY(d_dbg[0:rows, off:off + n], rb_sb[0:rows, 0:n], allow_slow_non_contiguous=True)
                mdump(sm8[:, 0:512], 4096, 512, rows=8)
                mdump(sq_sb[0:8, 136:137], 4608, 1, rows=8)
                mdump(sq_sb[0:1, 128:133], 4612, 5, rows=1)
                mdump(mk8[:, 0:512], 4620, 512, rows=8)
                mdump(pos8[:, 0:512], 5132, 512, rows=8)
            x3l = P.dmaG(x3s[:, :, 0:1024], d_x3bf.rearrange("(t p) c -> p t c", p=128))
            it1 = P.G.iota(i32_sb[:, 0:32], pattern=[[1, 32]], base=0, channel_multiplier=0)
            it2 = P.G.iota(i32_sb[:, 32:64], pattern=[[0, 32]], base=0, channel_multiplier=1)
            it3 = P.G.iota(i32_sb[:, 64:65], pattern=[[1, 1]], base=0, channel_multiplier=1)
            P.pS.inc(it3)
            P.V.wait_ge(P.pS.h, P.pS.n)
            vS_(P.V.tensor_copy(sq_sb[:, 137:169], i32_sb[:, 0:32]))
            vS_(P.V.tensor_copy(sq_sb[:, 169:201], i32_sb[:, 32:64]))
            vS_(P.V.tensor_copy(sq_sb[:, 201:202], i32_sb[:, 64:65]))
            P.barrier()
            for t in range(32):
                tp1 = P.T.transpose(P.pb[3][:, 0:8], pos8[:, t * 128:t * 128 + 128], idf_sb[0:8, 0:8])
                tp2 = P.T.transpose(P.pb[4][:, 0:8], sm8[:, t * 128:t * 128 + 128], idf_sb[0:8, 0:8])
                P.tS.inc(tp2)
                P.V.wait_ge(P.tS.h, P.tS.n)
                vS_(P.V.tensor_copy(dest_i[:, t, :], P.pb[3][:, 0:8]))
                vS_(P.V.tensor_copy(x3s[:, t, 1024:1032], P.pb[4][:, 0:8]))
                P.T.wait_ge(P.vS.h, P.vS.n)
            vS_(P.V.tensor_copy(x3s[:, :, 1032:1033], sq_sb[:, 137:169].rearrange("p (t o) -> p t o", o=1)))
            vS_(P.V.tensor_copy(x3s[:, :, 1033:1034], sq_sb[:, 169:201].rearrange("p (t o) -> p t o", o=1)))
            P.barrier()
            if MOE_STOP <= 5:
                return
            cmp_scope = nc.enter_named_scope("compact", False)
            scat_done = []
            for e in range(NEXP):
                for t in range(32):
                    sc1 = P.G.indirect_dma_start(
                        out=d_comp, out_offset=bass.IndirectOffsetOnAxis(ap=dest_i[:, t, e:e + 1], axis=0),
                        in_=x3s[:, t, :], in_offset=None)
                    P.qG.inc(sc1, 16)
                scat_done.append(P.qG.n)
            scat_done = [P.qG.n] * NEXP
            P.G.wait_ge(P.qG.h, P.qG.n)
            P.barrier()
            nc.leave_named_scope("compact", cmp_scope[0], False)
            if MOE_STOP <= 6:
                P.G.wait_ge(P.qG.h, P.qG.n)
                P.barrier()
                return
            wd3 = warena[:, 0:22528].rearrange("p (fc d) -> p fc d", d=1024)
            wu_db = warena[:, 22528:24576].rearrange("p (b c n) -> p b c n", b=2, c=8)
            gx = act_sb[:, 0:6144].rearrange("p (c n) -> p c n", c=8)
            ch9_sb = vkm_sb[:, 0:768]
            def mid_ap(fc):
                if fc < 21:
                    return ln_sb[:, fc * 768:(fc + 1) * 768]
                return vkm_sb[:, 0:768]
            g_r = st_sb[0:1, 0:768]
            hi_r = st_sb[0:1, 768:1536]
            lo_r = st_sb[0:1, 1536:2304]
            idx_r = st_sb[0:1, 2304:3072]
            gcols = sq_sb[:, 256:264]
            idxcols = i32_sb[:, 260:268]
            iotapf = sq_sb[:, 201:202]
            ohb = p_sb[:, 0:1]
            oh8 = p_sb[:, 1:2]
            oh9 = p_sb[:, 2:3]
            vS_(P.V.tensor_scalar(out=sq_sb[:, 202:203], in0=iotapf, scalar1=8.0, scalar2=None, op0=ALU.is_equal))
            vS_(P.V.tensor_scalar(out=sq_sb[:, 203:204], in0=iotapf, scalar1=9.0, scalar2=None, op0=ALU.is_equal))
            vS_(P.V.tensor_copy(oh8, sq_sb[:, 202:203]))
            vS_(P.V.tensor_copy(oh9, sq_sb[:, 203:204]))
            for e in range(NEXP):
              with nc.named_scope(f"exp{e}"):
                g = e // 2
                P.S.wait_ge(P.sS.h, P.sS.n)
                P.G.wait_ge(P.qG.h, scat_done[e])
                wl = P.dmaG(wd3[:, :, :], d_wdn[e].rearrange("(fc p) d -> p fc d", p=128))
                P.dmaA(sc_sb[:, 32:54], d_bup[g])
                bl = P.dmaA(sc_sb[:, 54:62], d_bdn[e])
                P.Y.wait_ge(P.qG.h, scat_done[e])
                for c9 in range(8):
                    P.dmaY(gx[:, c9, :], d_comp[e * C_CAP:(e + 1) * C_CAP, c9 * 128:c9 * 128 + 128], transpose=True)
                P.dmaY(ch9_sb, d_comp[e * C_CAP:(e + 1) * C_CAP, 1024:1152][:, 0:128], transpose=True)
                vS_(P.V.tensor_scalar(out=sq_sb[:, 204:205], in0=iotapf, scalar1=float(e), scalar2=None, op0=ALU.is_equal))
                vS_(P.V.tensor_copy(ohb, sq_sb[:, 204:205]))
                P.barrier()
                for h2 in range(2):
                    mg = P.T.matmul(P.pb[0][0:1, 0:384], ohb, ch9_sb[:, h2 * 384:h2 * 384 + 384], start=True, stop=True)
                    mh = P.T.matmul(P.pb[1][0:1, 0:384], oh8, ch9_sb[:, h2 * 384:h2 * 384 + 384], start=True, stop=True)
                    ml = P.T.matmul(P.pb[2][0:1, 0:384], oh9, ch9_sb[:, h2 * 384:h2 * 384 + 384], start=True, stop=True)
                    P.tS.inc(ml)
                    P.V.wait_ge(P.tS.h, P.tS.n)
                    vS_(P.V.tensor_copy(g_r[0:1, h2 * 384:h2 * 384 + 384], P.pb[0][0:1, 0:384]))
                    vS_(P.V.tensor_copy(hi_r[0:1, h2 * 384:h2 * 384 + 384], P.pb[1][0:1, 0:384]))
                    vS_(P.V.tensor_copy(lo_r[0:1, h2 * 384:h2 * 384 + 384], P.pb[2][0:1, 0:384]))
                    P.T.wait_ge(P.vS.h, P.vS.n)
                vS_(P.V.scalar_tensor_tensor(out=idx_r, in0=hi_r, scalar=128.0, in1=lo_r, op0=ALU.mult, op1=ALU.add))
                if dump and e == 0:
                    mdump(g_r[0:1, 0:512], 5648, 512, rows=1)
                    mdump(idx_r[0:1, 0:512], 6160, 512, rows=1)
                    mdump(gx[:, 0, 0:512], 6672, 512)
                P.T.wait_ge(P.vS.h, P.vS.n)
                for st6 in range(6):
                    tg = P.T.transpose(P.pb[3][:, 0:1], g_r[0:1, st6 * 128:st6 * 128 + 128], idf_sb[0:1, 0:1])
                    ti = P.T.transpose(P.pb[4][:, 0:1], idx_r[0:1, st6 * 128:st6 * 128 + 128], idf_sb[0:1, 0:1])
                    P.tS.inc(ti)
                    P.V.wait_ge(P.tS.h, P.tS.n)
                    vS_(P.V.tensor_copy(gcols[:, st6:st6 + 1], P.pb[3][:, 0:1]))
                    vS_(P.V.tensor_copy(idxcols[:, st6:st6 + 1], P.pb[4][:, 0:1]))
                    P.T.wait_ge(P.vS.h, P.vS.n)
                wuload = []
                wumm = {}
                for fc in range(2):
                    P.dmaG(wu_db[:, fc % 2, :, :], d_wup[g][:, fc * 128:fc * 128 + 128].rearrange("(c p) n -> p c n", p=128))
                    wuload.append(P.qG.n)
                for fc in range(22):
                    slot = fc % 2
                    P.T.wait_ge(P.qG.h, wuload[fc])
                    for h2 in range(2):
                        bank = P.pb[(fc * 2 + h2) % 2]
                        P.T.wait_ge(P.sS.h, P.sS.n)
                        mm = None
                        for ch in range(NCH):
                            mm = P.T.matmul(bank[:, 0:384], wu_db[:, slot, ch, :], gx[:, ch, h2 * 384:h2 * 384 + 384], start=(ch == 0), stop=(ch == NCH - 1))
                        wumm[fc] = P.tS.inc(mm)
                        P.S.wait_ge(P.tS.h, P.tS.n)
                        sv = P.S.activation(out=mid_ap(fc)[:, h2 * 384:h2 * 384 + 384], in_=bank[:, 0:384], func=AF.Silu, bias=sc_sb[:, 32 + fc:33 + fc], scale=1.0)
                        P.sS.inc(sv)
                        if h2 == 1 and fc + 2 < 22:
                            P.G.wait_ge(P.tS.h, wumm[fc])
                            P.dmaG(wu_db[:, fc % 2, :, :], d_wup[g][:, (fc + 2) * 128:(fc + 2) * 128 + 128].rearrange("(c p) n -> p c n", p=128))
                            wuload.append(P.qG.n)
                        if dump and e == 0 and fc == 0 and h2 == 0:
                            P.V.wait_ge(P.sS.h, P.sS.n)
                            mdump(wu_db[:, slot, 0, :], 7184, 128)
                            mdump(mid_ap(0)[:, 0:384], 7312, 384)
                P.T.wait_ge(P.sS.h, P.sS.n)
                P.T.wait_ge(P.qG.h, P.qG.n)
                P.T.wait_ge(P.qA.h, P.qA.n)
                for stp in range(3):
                    for dc in range(NCH):
                        bank = P.pb[dc % 2]
                        P.T.wait_ge(P.vS.h, P.vS.n)
                        mm = None
                        for fc in range(22):
                            mm = P.T.matmul(bank[:, 0:256], wd3[:, fc, dc * 128:dc * 128 + 128], mid_ap(fc)[:, stp * 256:stp * 256 + 256], start=(fc == 0), stop=(fc == 21))
                        P.tS.inc(mm)
                        P.V.wait_ge(P.tS.h, P.tS.n)
                        vS_(P.V.tensor_scalar(out=rb_sb[:, 0:256], in0=bank[:, 0:256], scalar1=sc_sb[:, 54 + dc:55 + dc], scalar2=None, op0=ALU.add))
                        if dump and e == 0 and stp == 0 and dc == 0:
                            dv = P.V.tensor_copy(ln_extra[:, 0:256], rb_sb[:, 0:256])
                            P.vS.inc(dv)
                            P.Y.wait_ge(P.vS.h, P.vS.n)
                            P.dmaY(d_dbg[:, 7824:8080], ln_extra[:, 0:256])
                        P.T.wait_ge(P.vS.h, P.vS.n)
                        for ti in range(2):
                            tt = P.T.transpose(P.pb[2 + ti][:, 0:128], rb_sb[:, ti * 128:ti * 128 + 128], idf_sb[:, :])
                            P.tS.inc(tt)
                            P.V.wait_ge(P.tS.h, P.tS.n)
                            vS_(P.V.tensor_scalar(out=xr_sb[:, ti * 1024 + dc * 128: ti * 1024 + dc * 128 + 128], in0=P.pb[2 + ti][:, 0:128], scalar1=gcols[:, stp * 2 + ti:stp * 2 + ti + 1], scalar2=None, op0=ALU.mult))
                            P.T.wait_ge(P.vS.h, P.vS.n)
                    P.G.wait_ge(P.vS.h, P.vS.n)
                    for ti in range(2):
                        sa = P.G.indirect_dma_start(
                            out=d_y, out_offset=bass.IndirectOffsetOnAxis(ap=idxcols[:, stp * 2 + ti:stp * 2 + ti + 1], axis=0),
                            in_=xr_sb[:, ti * 1024:ti * 1024 + 1024], in_offset=None,
                            compute_op=ALU.add)
                        P.qG.inc(sa, 16)
                    P.G.wait_ge(P.qG.h, P.qG.n)
                    P.V.wait_ge(P.qG.h, P.qG.n)
                P.barrier()

        if n_sub >= 1:
            with nc.named_scope("L0_attn"):
                attn_layer(0)
        if n_sub >= 2:
            with nc.named_scope("L0_ffn"):
                ffn0_layer()
        if n_sub >= 3:
            with nc.named_scope("L1_attn"):
                attn_layer(2)
        if n_sub >= 4:
            with nc.named_scope("L1_moe"):
                moe_layer()
        if dump:
            P.barrier()
            for ch in range(NCH):
                P.dmaY(d_dump.rearrange("(c p) t -> c p t", p=128)[ch], xres3(ch))

        P.barrier()
    es.close()
    return nc


def _prep_shared(inputs):
    g = {}
    wq0 = inputs["L0_attn_W_qkv"]
    b0 = inputs["L0_attn_b_qkv"]
    cols = []
    bcols = []
    for p in range(8):
        for gsel in range(3):
            for h in (2 * p, 2 * p + 1):
                cols.append(wq0[:, h * 192 + gsel * 64: h * 192 + gsel * 64 + 64])
                bcols.append(b0[h * 192 + gsel * 64: h * 192 + gsel * 64 + 64])
    g["w_l0qkv"] = np.ascontiguousarray(np.concatenate(cols, axis=1))
    bl0 = np.stack([np.concatenate(bcols[i * 2:i * 2 + 2]) for i in range(24)], axis=1)
    g["b_l0qkv"] = np.ascontiguousarray(bl0)
    g["w_l0out"] = inputs["L0_attn_W_out"]
    g["b_l0out"] = np.ascontiguousarray(inputs["L0_attn_b_out"].reshape(8, 128).T)
    wq1 = np.concatenate([inputs["L1_attn_wq"], inputs["L1_attn_wk"], inputs["L1_attn_wv"]], axis=1)
    cols = []
    for p in range(8):
        for gsel in range(3):
            for h in (2 * p, 2 * p + 1):
                cols.append(wq1[:, gsel * 1024 + h * 64: gsel * 1024 + h * 64 + 64])
    g["w_l1qkv"] = np.ascontiguousarray(np.concatenate(cols, axis=1))
    g["w_l1out"] = inputs["L1_attn_wo"]
    g["w_l1ph"] = inputs["L1_attn_w_phase"]
    g["l1_carrier"] = np.ascontiguousarray(inputs["L1_attn_carrier"].reshape(16, 1))
    g["l1_lock"] = np.ascontiguousarray(inputs["L1_attn_lock"].reshape(16, 1))
    g["l1_bw"] = np.ascontiguousarray(inputs["L1_attn_bandwidth"].reshape(16, 1))
    g["l1_carrier_r"] = np.ascontiguousarray(inputs["L1_attn_carrier"].reshape(1, 16))
    g["l1_bw_r"] = np.ascontiguousarray(inputs["L1_attn_bandwidth"].reshape(1, 16))
    g["l1_amask"] = np.ascontiguousarray(inputs["L1_attn_active_mask"].reshape(16, 1)).view(np.uint8)
    g["f0_proj"] = inputs["L0_ffn_proj"]
    g["f0_c1"] = inputs["L0_ffn_C1"]
    g["f0_c2"] = inputs["L0_ffn_C2"]
    g["f0_win1"] = np.ascontiguousarray(inputs["L0_ffn_Win1"][0:1024])
    g["f0_b1"] = np.ascontiguousarray(inputs["L0_ffn_Win1"][1024].reshape(4, 128).T)
    g["f0_wout1t"] = np.ascontiguousarray(inputs["L0_ffn_Wout1"].T)
    g["f0_win2"] = np.ascontiguousarray(inputs["L0_ffn_Win2"][0:4096])
    g["f0_b2"] = np.ascontiguousarray(inputs["L0_ffn_Win2"][4096].reshape(4, 128).T)
    g["f0_wout2t"] = np.ascontiguousarray(inputs["L0_ffn_Wout2"].T)
    g["m_wr"] = inputs["L1_ffn_Wr"]
    g["m_br"] = np.ascontiguousarray(inputs["L1_ffn_br"].reshape(8, 1))
    wup = np.zeros((4, D, FUP), np.float32)
    wup[:, :, :2730] = inputs["L1_ffn_W_up"]
    g["m_wup"] = wup
    bup = np.zeros((4, FUP), np.float32)
    bup[:, :2730] = inputs["L1_ffn_b_up"]
    g["m_bup"] = np.ascontiguousarray(bup.reshape(4, FUP // 128, 128).transpose(0, 2, 1))
    wdn = np.zeros((NEXP, FUP, D), np.float32)
    wdn[:, :2730, :] = inputs["L1_ffn_W_down"]
    g["m_wdn"] = wdn
    g["m_bdn"] = np.ascontiguousarray(inputs["L1_ffn_b_down"].reshape(NEXP, 8, 128).transpose(0, 2, 1))
    g["c_ident"] = np.eye(128, dtype=BF)
    g["c_identf"] = np.eye(128, dtype=np.float32)
    jj, kk = np.meshgrid(np.arange(128), np.arange(128), indexing="ij")
    g["c_lneg"] = ((kk > jj) * -1e30).astype(BF)
    e2 = np.zeros((2, 128), np.float32)
    e2[0, 0:64] = 1.0
    e2[1, 64:128] = 1.0
    g["c_e2"] = e2
    eh = np.zeros((16, 128), np.float32)
    for h in range(16):
        eh[h, (h % 2) * 64:(h % 2) * 64 + 64] = 1.0
    g["c_ehalf"] = eh
    s168 = np.zeros((16, 8), np.float32)
    for h in range(16):
        s168[h, h // 2] = 1.0
    g["c_sel168"] = s168
    g["c_ecol"] = np.arange(8, dtype=np.float32).reshape(8, 1) * C_CAP - 1.0
    return g


def make_in_maps(inputs):
    g = _prep_shared(inputs)
    x = inputs["L0_attn_X"]
    in_maps = []
    for c in range(NCORE):
        m = dict(g)
        m["x0_fm"] = np.ascontiguousarray(x[2 * c:2 * c + 2].reshape(T, D).T)
        m["bisrows"] = np.array([(r * 8 + c) * 4 + q for r in range(8) for q in range(4)], dtype=np.int32).reshape(32, 1)
        in_maps.append(m)
    return in_maps


def kernel(**inputs):
    in_maps = make_in_maps(inputs)
    nc = build_program()
    res = run_bass_kernel_spmd(nc, in_maps, core_ids=list(range(NCORE)))
    y = np.concatenate([res.results[c]["y"] for c in range(NCORE)], axis=0)
    return y.reshape(16, 2048, D)


# version 07
# speedup vs baseline: 1.0633x; 1.0633x over previous
import numpy as np
import ml_dtypes
import concourse.bass as bass
import concourse.mybir as mybir
from concourse.bass_utils import run_bass_kernel_spmd

F32 = mybir.dt.float32
BF16 = mybir.dt.bfloat16
I32 = mybir.dt.int32
U8 = mybir.dt.uint8
AF = mybir.ActivationFunctionType
ALU = mybir.AluOpType

NCORE = 8
T = 4096
SEQ = 2048
D = 1024
NCH = 8
DH = 64
C_CAP = 768
FUP = 2816
NEXP = 8
N_SUB = 4
DUMP_XFM = False
import os
L1_SKIP = set(os.environ.get("L1_SKIP", "").split(","))
MOE_STOP = int(os.environ.get("MOE_STOP", "99"))

BF = ml_dtypes.bfloat16


class Sem:
    def __init__(self, h):
        self.h = h
        self.n = 0

    def inc(self, instr, k=1):
        instr.then_inc(self.h, k)
        self.n += k
        return self.n


class Prog:
    def __init__(self, nc, es):
        self.nc = nc
        S = es.enter_context
        self.T = nc.tensor
        self.V = nc.vector
        self.S = nc.scalar
        self.G = nc.gpsimd
        self.Y = nc.sync
        self.qY = Sem(S(nc.semaphore("qY")))
        self.qG = Sem(S(nc.semaphore("qG")))
        self.qA = Sem(S(nc.semaphore("qA")))
        self.tS = Sem(S(nc.semaphore("tS")))
        self.vS = Sem(S(nc.semaphore("vS")))
        self.sS = Sem(S(nc.semaphore("sS")))
        self.pS = Sem(S(nc.semaphore("pS")))
        self.ccS = Sem(S(nc.semaphore("ccS")))
        self.pb = [S(nc.psum_tensor(f"pb{i}", [128, 512], F32)) for i in range(6)]
        self.pbbA = S(nc.psum_tensor("pbbA", [128, 1024], BF16))
        self.pbbB = S(nc.psum_tensor("pbbB", [128, 1024], BF16))
        self.bar_v = S(nc.sbuf_tensor("bar_v", [1, 2], F32))
        self.bar_s = S(nc.sbuf_tensor("bar_s", [1, 2], F32))
        self.bar_g = S(nc.sbuf_tensor("bar_g", [1, 2], F32))

    def dmaY(self, out, in_, **kw):
        i = self.Y.dma_start(out=out, in_=in_, **kw)
        self.qY.inc(i, 16)
        return i

    def dmaG(self, out, in_, **kw):
        i = self.G.dma_start(out=out, in_=in_, **kw)
        self.qG.inc(i, 16)
        return i

    def dmaA(self, out, in_, **kw):
        i = self.S.dma_start(out=out, in_=in_, **kw)
        self.qA.inc(i, 16)
        return i

    def barrier(self, dma=True):
        self.V.memset(self.bar_v[:], 0.0).then_inc(self.vS.h, 1)
        self.vS.n += 1
        self.S.activation(out=self.bar_s[:], in_=self.bar_s[:], func=AF.Identity).then_inc(self.sS.h, 1)
        self.sS.n += 1
        self.G.memset(self.bar_g[:], 0.0).then_inc(self.pS.h, 1)
        self.pS.n += 1
        self.T.matmul(self.pb[0][0:2, 0:2], self.bar_v[:], self.bar_v[:], start=True, stop=True).then_inc(self.tS.h, 1)
        self.tS.n += 1
        waits = [(self.tS, self.tS.n), (self.vS, self.vS.n), (self.sS, self.sS.n), (self.pS, self.pS.n)]
        if dma:
            waits += [(self.qY, self.qY.n), (self.qG, self.qG.n), (self.qA, self.qA.n)]
        for eng in (self.T, self.V, self.S, self.G, self.Y):
            for sem, v in waits:
                if v > 0:
                    eng.wait_ge(sem.h, v)


def build_program(n_sub=N_SUB, dump=DUMP_XFM):
    from contextlib import ExitStack
    nc = bass.Bass("TRN2", target_bir_lowering=False)
    es = ExitStack()
    Sx = es.enter_context

    d_x0 = nc.dram_tensor("x0_fm", [D, T], F32, kind="ExternalInput").ap()
    d_wl0 = nc.dram_tensor("w_l0qkv", [D, 3072], F32, kind="ExternalInput").ap()
    d_bl0 = nc.dram_tensor("b_l0qkv", [128, 24], F32, kind="ExternalInput").ap()
    d_wo0 = nc.dram_tensor("w_l0out", [D, D], F32, kind="ExternalInput").ap()
    d_bo0 = nc.dram_tensor("b_l0out", [128, 8], F32, kind="ExternalInput").ap()
    d_proj = nc.dram_tensor("f0_proj", [D, 16], F32, kind="ExternalInput").ap()
    d_c1 = nc.dram_tensor("f0_c1", [16, 512], F32, kind="ExternalInput").ap()
    d_c2 = nc.dram_tensor("f0_c2", [16, 512], F32, kind="ExternalInput").ap()
    d_win1 = nc.dram_tensor("f0_win1", [1024, 512], F32, kind="ExternalInput").ap()
    d_wout1t = nc.dram_tensor("f0_wout1t", [512, 4096], F32, kind="ExternalInput").ap()
    d_win2 = nc.dram_tensor("f0_win2", [4096, 512], F32, kind="ExternalInput").ap()
    d_wout2t = nc.dram_tensor("f0_wout2t", [512, D], F32, kind="ExternalInput").ap()
    d_fb1 = nc.dram_tensor("f0_b1", [128, 4], F32, kind="ExternalInput").ap()
    d_fb2 = nc.dram_tensor("f0_b2", [128, 4], F32, kind="ExternalInput").ap()
    d_wl1 = nc.dram_tensor("w_l1qkv", [D, 3072], F32, kind="ExternalInput").ap()
    d_wo1 = nc.dram_tensor("w_l1out", [D, D], F32, kind="ExternalInput").ap()
    d_wph = nc.dram_tensor("w_l1ph", [D, 2], F32, kind="ExternalInput").ap()
    d_carrier = nc.dram_tensor("l1_carrier", [16, 1], F32, kind="ExternalInput").ap()
    d_lock = nc.dram_tensor("l1_lock", [16, 1], F32, kind="ExternalInput").ap()
    d_bw = nc.dram_tensor("l1_bw", [16, 1], F32, kind="ExternalInput").ap()
    d_amask = nc.dram_tensor("l1_amask", [16, 1], U8, kind="ExternalInput").ap()
    d_carrier_r = nc.dram_tensor("l1_carrier_r", [1, 16], F32, kind="ExternalInput").ap()
    d_bw_r = nc.dram_tensor("l1_bw_r", [1, 16], F32, kind="ExternalInput").ap()
    d_wr = nc.dram_tensor("m_wr", [D, 8], F32, kind="ExternalInput").ap()
    d_br = nc.dram_tensor("m_br", [8, 1], F32, kind="ExternalInput").ap()
    d_wup = nc.dram_tensor("m_wup", [4, D, FUP], F32, kind="ExternalInput").ap()
    d_bup = nc.dram_tensor("m_bup", [4, 128, FUP // 128], F32, kind="ExternalInput").ap()
    d_wdn = nc.dram_tensor("m_wdn", [NEXP, FUP, D], F32, kind="ExternalInput").ap()
    d_bdn = nc.dram_tensor("m_bdn", [NEXP, 128, 8], F32, kind="ExternalInput").ap()
    d_ident = nc.dram_tensor("c_ident", [128, 128], BF16, kind="ExternalInput").ap()
    d_identf = nc.dram_tensor("c_identf", [128, 128], F32, kind="ExternalInput").ap()
    d_lneg = nc.dram_tensor("c_lneg", [128, 128], BF16, kind="ExternalInput").ap()
    d_e2 = nc.dram_tensor("c_e2", [2, 128], F32, kind="ExternalInput").ap()
    d_ehalf = nc.dram_tensor("c_ehalf", [16, 128], F32, kind="ExternalInput").ap()
    d_sel168 = nc.dram_tensor("c_sel168", [16, 8], F32, kind="ExternalInput").ap()

    d_y = nc.dram_tensor("y", [T, D], F32, kind="ExternalOutput").ap()
    d_xres = nc.dram_tensor("i_xres", [D, T], F32).ap()
    d_x3bf = nc.dram_tensor("i_x3bf", [T, D], BF16).ap()
    d_smi = nc.dram_tensor("i_smi", [8, T], F32)
    d_smo = nc.dram_tensor("i_smo", [64, T], F32)
    d_taui = nc.dram_tensor("i_taui", [1, 16], F32)
    d_tauo = nc.dram_tensor("i_tauo", [8, 16], F32)
    d_comp = nc.dram_tensor("i_comp", [NEXP * C_CAP + 8, 1152], BF16).ap()
    d_bisrows = nc.dram_tensor("bisrows", [32, 1], I32, kind="ExternalInput").ap()
    d_ecol = nc.dram_tensor("c_ecol", [8, 1], F32, kind="ExternalInput").ap()
    d_ydown = nc.dram_tensor("i_ydown", [C_CAP, D], F32).ap()
    if dump:
        d_dump = nc.dram_tensor("xdump", [D, T], F32, kind="ExternalOutput").ap()
        d_dbg = nc.dram_tensor("dbg", [128, 16 * 512], F32, kind="ExternalOutput").ap()

    def xres3(ch):
        return d_xres.rearrange("(c p) t -> c p t", p=128)[ch]

    warena = Sx(nc.sbuf_tensor("warena", [128, 36864], BF16))
    ln_sb = Sx(nc.sbuf_tensor("ln_sb", [128, 8 * SEQ], BF16))
    act_sb = Sx(nc.sbuf_tensor("act_sb", [128, 8 * SEQ], BF16))
    q_sb = Sx(nc.sbuf_tensor("q_sb", [128, SEQ], BF16))
    k_sb = Sx(nc.sbuf_tensor("k_sb", [128, SEQ], BF16))
    vfm_sb = Sx(nc.sbuf_tensor("vfm_sb", [128, SEQ], BF16))
    vkm_sb = Sx(nc.sbuf_tensor("vkm_sb", [128, 16 * 130], BF16))
    p_sb = Sx(nc.sbuf_tensor("p_sb", [128, 4 * 512], BF16))
    xr_sb = Sx(nc.sbuf_tensor("xr_sb", [128, 4 * 512], F32))
    sq_sb = Sx(nc.sbuf_tensor("sq_sb", [128, 512], F32))
    rb_sb = Sx(nc.sbuf_tensor("rb_sb", [128, 512], F32))
    st_sb = Sx(nc.sbuf_tensor("st_sb", [1, 8192], F32))
    sc_sb = Sx(nc.sbuf_tensor("sc_sb", [128, 64], F32))
    on_sb = Sx(nc.sbuf_tensor("on_sb", [128, 4], F32))
    onesr_sb = Sx(nc.sbuf_tensor("onesr_sb", [128, 128], F32))
    onesb_sb = Sx(nc.sbuf_tensor("onesb_sb", [128, 64], BF16))
    id_sb = Sx(nc.sbuf_tensor("id_sb", [128, 128], BF16))
    idf_sb = Sx(nc.sbuf_tensor("idf_sb", [128, 128], F32))
    lneg_sb = Sx(nc.sbuf_tensor("lneg_sb", [128, 128], BF16))
    mk_u8 = Sx(nc.sbuf_tensor("mk_u8", [128, 4], U8))
    ln_extra = Sx(nc.sbuf_tensor("ln_extra", [128, 512], F32))
    ent_sb = Sx(nc.sbuf_tensor("ent_sb", [128, 608], F32))
    i32_sb = Sx(nc.sbuf_tensor("i32_sb", [128, 268], I32))

    P = Prog(nc, es)
    sqb_sb = sq_sb[:, :].bitcast(BF16)

    def vS_(i):
        P.vS.inc(i)
        P.V.wait_ge(P.vS.h, P.vS.n)
        return i
    ones_col = on_sb[:, 0:1]
    onesrow = onesr_sb[0:1, :]

    with nc.Block():
        P.dmaY(id_sb[:], d_ident)
        P.dmaY(idf_sb[:], d_identf)
        P.dmaY(lneg_sb[:], d_lneg)
        for ch in range(NCH):
            P.dmaY(xres3(ch), d_x0.rearrange("(c p) t -> c p t", p=128)[ch])
        mz = P.V.memset(on_sb[:], 1.0)
        mz2 = P.V.memset(onesr_sb[:], 1.0)
        P.V.memset(onesb_sb[:, 0:64], 1.0)
        zc = P.V.memset(p_sb[:, :], 0.0)
        P.vS.inc(zc)
        P.Y.wait_ge(P.vS.h, P.vS.n)
        if n_sub >= 4:
            for tt in range(48):
                P.dmaY(d_comp[tt * 128:(tt + 1) * 128, :], p_sb[:, 0:1152])
            P.dmaY(d_comp[6144:6152, :], p_sb[0:8, 0:1152])
        P.barrier()

        def ln_stats(ts0, tlen=SEQ):
            nch_total = (tlen // 512) * NCH
            loads = []
            mmdone = {}
            def issue_load(i):
                nb, ch = divmod(i, NCH)
                buf = xr_sb[:, (i % 2) * 512:(i % 2) * 512 + 512]
                if i >= 2:
                    P.Y.wait_ge(P.tS.h, mmdone[i - 2])
                P.dmaY(buf, xres3(ch)[:, ts0 + nb * 512:ts0 + nb * 512 + 512])
                loads.append(P.qY.n)
                return buf
            bufs = {}
            for i in range(min(2, nch_total)):
                bufs[i] = issue_load(i)
            for i in range(nch_total):
                nb, ch = divmod(i, NCH)
                buf = bufs[i]
                P.S.wait_ge(P.qY.h, loads[i])
                P.S.wait_ge(P.tS.h, P.tS.n)
                sqv = P.S.activation(out=sqb_sb[:, 0:512], in_=buf, func=AF.Square)
                P.sS.inc(sqv)
                P.V.wait_ge(P.qY.h, loads[i])
                P.V.wait_ge(P.tS.h, P.tS.n)
                cbv = P.V.tensor_copy(sqb_sb[:, 512:1024], buf)
                P.vS.inc(cbv)
                P.T.wait_ge(P.vS.h, P.vS.n)
                P.T.matmul(P.pb[0][0:1, :], onesb_sb[:, 0:1], sqb_sb[:, 512:1024], start=(ch == 0), stop=(ch == NCH - 1))
                P.T.wait_ge(P.sS.h, P.sS.n)
                mm2 = P.T.matmul(P.pb[1][0:1, :], onesb_sb[:, 0:1], sqb_sb[:, 0:512], start=(ch == 0), stop=(ch == NCH - 1))
                mmdone[i] = P.tS.inc(mm2)
                if i + 2 < nch_total:
                    bufs[i + 2] = issue_load(i + 2)
                if ch == NCH - 1:
                    o0 = nb * 512
                    P.V.wait_ge(P.tS.h, P.tS.n)
                    P.V.tensor_scalar(out=st_sb[0:1, o0:o0 + 512], in0=P.pb[0][0:1, :], scalar1=1.0 / D, scalar2=None, op0=ALU.mult)
                    c2 = P.V.tensor_scalar(out=st_sb[0:1, SEQ + o0:SEQ + o0 + 512], in0=P.pb[1][0:1, :], scalar1=1.0 / D, scalar2=None, op0=ALU.mult)
                    P.vS.inc(c2)
                    P.T.wait_ge(P.vS.h, P.vS.n)
            m = st_sb[0:1, 0:SEQ]
            msq = st_sb[0:1, SEQ:2 * SEQ]
            nmr = st_sb[0:1, 2 * SEQ:3 * SEQ]
            m2row = st_sb[0:1, 3 * SEQ:4 * SEQ]
            vS_(P.V.tensor_tensor(out=m2row, in0=m, in1=m, op=ALU.mult))
            vS_(P.V.tensor_tensor(out=msq, in0=msq, in1=m2row, op=ALU.subtract))
            v1 = vS_(P.V.tensor_scalar(out=msq, in0=msq, scalar1=1e-5, scalar2=None, op0=ALU.add))
            P.S.wait_ge(P.vS.h, P.vS.n)
            s1 = P.S.activation(out=msq, in_=msq, func=AF.Sqrt)
            P.sS.inc(s1)
            P.V.wait_ge(P.sS.h, P.sS.n)
            vS_(P.V.reciprocal(out=msq, in_=msq))
            vS_(P.V.tensor_tensor(out=nmr, in0=m, in1=msq, op=ALU.mult))
            v3 = vS_(P.V.tensor_scalar(out=nmr, in0=nmr, scalar1=-1.0, scalar2=None, op0=ALU.mult))

        def ln_apply(ts0, t0, tn, out3, out_off=0):
            r = st_sb[0:1, SEQ:2 * SEQ]
            nmr = st_sb[0:1, 2 * SEQ:3 * SEQ]
            for nb in range(tn // 512):
                tb = t0 + nb * 512
                so = tb - ts0
                P.T.wait_ge(P.vS.h, P.vS.n)
                P.T.wait_ge(P.sS.h, P.sS.n)
                P.T.matmul(P.pb[2][:, :], onesrow, r[0:1, so:so + 512], start=True, stop=True)
                m2 = P.T.matmul(P.pb[3][:, :], onesrow, nmr[0:1, so:so + 512], start=True, stop=True)
                P.tS.inc(m2)
                for ch in range(NCH):
                    buf = xr_sb[:, (2 + (ch % 2)) * 512:(2 + (ch % 2)) * 512 + 512]
                    P.Y.wait_ge(P.vS.h, P.vS.n)
                    P.dmaY(buf, xres3(ch)[:, tb:tb + 512])
                    P.V.wait_ge(P.qY.h, P.qY.n)
                    P.V.wait_ge(P.tS.h, P.tS.n)
                    P.V.tensor_tensor(out=ln_extra[:, 0:512], in0=buf, in1=P.pb[2][:, :], op=ALU.mult)
                    vv = P.V.tensor_tensor(out=out3[:, ch, out_off + nb * 512:out_off + nb * 512 + 512], in0=ln_extra[:, 0:512], in1=P.pb[3][:, :], op=ALU.add)
                    P.vS.inc(vv)

        def qkv_pair(lidx, pr, ln3, wq):
            base = pr * 384
            for g, out_sbuf in ((0, q_sb), (1, k_sb), (2, vfm_sb)):
                for nb in range(4):
                    bank = P.pb[4 + (nb % 2)]
                    P.T.wait_ge(P.sS.h, P.sS.n)
                    mm = None
                    for ch in range(NCH):
                        mm = P.T.matmul(bank[:, :], wq[:, ch, base + g * 128: base + g * 128 + 128],
                                        ln3[:, ch, nb * 512:nb * 512 + 512], start=(ch == 0), stop=(ch == NCH - 1))
                    P.tS.inc(mm)
                    P.S.wait_ge(P.tS.h, P.tS.n)
                    bias_col = sc_sb[:, pr * 3 + g: pr * 3 + g + 1]
                    ev = P.S.activation(out=out_sbuf[:, nb * 512:nb * 512 + 512], in_=bank[:, :], func=AF.Identity, bias=bias_col, scale=1.0)
                    P.sS.inc(ev)
            P.T.wait_ge(P.sS.h, P.sS.n)

        def v_transpose(pr):
            for kt in range(16):
                P.T.transpose(P.pbbA[:, 0:64], vfm_sb[0:64, kt * 128:kt * 128 + 128], id_sb[0:64, 0:64])
                m2 = P.T.transpose(P.pbbB[:, 0:64], vfm_sb[64:128, kt * 128:kt * 128 + 128], id_sb[64:128, 64:128])
                P.tS.inc(m2)
                P.V.wait_ge(P.tS.h, P.tS.n)
                P.V.tensor_copy(vkm_sb[:, kt * 130:kt * 130 + 64], P.pbbA[:, 0:64])
                e2 = P.V.tensor_copy(vkm_sb[:, kt * 130 + 65:kt * 130 + 129], P.pbbB[:, 0:64])
                P.vS.inc(e2)
                P.T.wait_ge(P.vS.h, P.vS.n)
            vk3 = vkm_sb[:, :].rearrange("p (k c) -> p k c", c=130)
            P.V.memset(vk3[:, :, 64:65], 1.0)
            o2 = P.V.memset(vk3[:, :, 129:130], 1.0)
            P.vS.inc(o2)

        def scores_av(lidx, s, pr, qb, lng3):
            causal = lidx == 2
            ktmax = (4 * qb + 4) if causal else 16
            q0 = qb * 512
            evac_done = {}
            mm_done = {}
            moffs = {}

            def emit_scores(kt):
                diag = causal and kt >= 4 * qb
                moff = (kt - 4 * qb) * 128 if diag else 0
                bA = P.pb[0 + 2 * (kt % 2)]
                bB = P.pb[1 + 2 * (kt % 2)]
                if kt >= 2:
                    P.T.wait_ge(P.sS.h, evac_done[kt - 2])
                P.T.matmul(bA[:, moff:512], k_sb[0:64, kt * 128:kt * 128 + 128], q_sb[0:64, q0 + moff:q0 + 512],
                           start=True, stop=not diag)
                if diag:
                    P.T.matmul(bA[:, moff:moff + 128], lneg_sb[:], id_sb[:], start=False, stop=True)
                last = P.T.matmul(bB[:, moff:512], k_sb[64:128, kt * 128:kt * 128 + 128], q_sb[64:128, q0 + moff:q0 + 512],
                                  start=True, stop=not diag)
                if diag:
                    last = P.T.matmul(bB[:, moff:moff + 128], lneg_sb[:], id_sb[:], start=False, stop=True)
                mm_done[kt] = P.tS.inc(last)
                moffs[kt] = moff

            emit_scores(0)
            if ktmax > 1:
                emit_scores(1)
            for kt in range(ktmax):
                moff = moffs[kt]
                P.S.wait_ge(P.tS.h, mm_done[kt])
                bA = P.pb[0 + 2 * (kt % 2)]
                bB = P.pb[1 + 2 * (kt % 2)]
                pA = p_sb[:, (kt % 2) * 1024:(kt % 2) * 1024 + 512]
                pB = p_sb[:, (kt % 2) * 1024 + 512:(kt % 2) * 1024 + 1024]
                if causal:
                    biasA = lng3[:, s * 16 + kt, 2 * pr:2 * pr + 1]
                    biasB = lng3[:, s * 16 + kt, 2 * pr + 1:2 * pr + 2]
                else:
                    biasA = 0.0
                    biasB = 0.0
                P.S.activation(out=pA[:, moff:512], in_=bA[:, moff:512], func=AF.Exp, scale=0.125, bias=biasA)
                e2 = P.S.activation(out=pB[:, moff:512], in_=bB[:, moff:512], func=AF.Exp, scale=0.125, bias=biasB)
                evac_done[kt] = P.sS.inc(e2)
                P.T.wait_ge(P.sS.h, evac_done[kt])
                P.T.matmul(P.pb[4][0:65, moff:512], vkm_sb[:, kt * 130:kt * 130 + 65], pA[:, moff:512],
                           start=(kt == 0), stop=(kt == ktmax - 1))
                avB = P.T.matmul(P.pb[5][0:65, moff:512], vkm_sb[:, kt * 130 + 65:kt * 130 + 130], pB[:, moff:512],
                                 start=(kt == 0), stop=(kt == ktmax - 1))
                P.tS.inc(avB)
                if kt + 2 < ktmax:
                    emit_scores(kt + 2)

        def divide_store(pr, qb, ysb):
            P.S.wait_ge(P.tS.h, P.tS.n)
            rra = sqb_sb[0:1, 0:512]
            rrb = sqb_sb[0:1, 512:1024]
            l1i = P.S.activation(out=ln_extra[0:1, 0:512], in_=P.pb[4][64:65, :], func=AF.Ln)
            P.sS.inc(l1i)
            P.S.wait_ge(P.sS.h, P.sS.n)
            P.S.activation(out=rra, in_=ln_extra[0:1, 0:512], func=AF.Exp, scale=-1.0)
            l2i = P.S.activation(out=ln_extra[0:1, 0:512], in_=P.pb[5][64:65, :], func=AF.Ln)
            P.sS.inc(l2i)
            P.S.wait_ge(P.sS.h, P.sS.n)
            r2 = P.S.activation(out=rrb, in_=ln_extra[0:1, 0:512], func=AF.Exp, scale=-1.0)
            P.sS.inc(r2)
            P.T.wait_ge(P.sS.h, P.sS.n)
            P.T.matmul(P.pb[0][0:64, :], onesb_sb[0:1, 0:64], rra, start=True, stop=True, tile_position=(0, 0))
            b2 = P.T.matmul(P.pb[0][64:128, :], onesb_sb[0:1, 0:64], rrb, start=True, stop=True, tile_position=(0, 64))
            P.tS.inc(b2)
            P.V.wait_ge(P.tS.h, P.tS.n)
            c1 = P.V.tensor_copy(rb_sb[:, :], P.pb[0][:, :])
            P.vS.inc(c1)
            P.V.tensor_tensor(out=ysb[0:64, pr, qb * 512:qb * 512 + 512], in0=P.pb[4][0:64, :], in1=rb_sb[0:64, :], op=ALU.mult)
            t2 = P.V.tensor_tensor(out=ysb[64:128, pr, qb * 512:qb * 512 + 512], in0=P.pb[5][0:64, :], in1=rb_sb[64:128, :], op=ALU.mult)
            P.vS.inc(t2)
            P.T.wait_ge(P.vS.h, P.vS.n)

        def outproj_residual(lidx, s, ysb, wo):
            ts0 = s * SEQ
            for qb in range(4):
                for f in range(NCH):
                    bank = P.pb[0 + (f % 2)]
                    P.T.wait_ge(P.vS.h, P.vS.n)
                    for ch in range(NCH):
                        mm = P.T.matmul(bank[:, :], wo[:, ch, f * 128:f * 128 + 128], ysb[:, ch, qb * 512:qb * 512 + 512],
                                        start=(ch == 0), stop=(ch == NCH - 1))
                    P.tS.inc(mm)
                    xbuf = xr_sb[:, (f % 2) * 512:(f % 2) * 512 + 512]
                    P.Y.wait_ge(P.vS.h, P.vS.n)
                    P.dmaY(xbuf, xres3(f)[:, ts0 + qb * 512: ts0 + qb * 512 + 512])
                    P.V.wait_ge(P.tS.h, P.tS.n)
                    P.V.wait_ge(P.qY.h, P.qY.n)
                    bias_col = sc_sb[:, 24 + f:25 + f]
                    xout = xr_sb[:, (2 + f % 2) * 512:(2 + f % 2) * 512 + 512]
                    st1 = P.V.scalar_tensor_tensor(out=xout, in0=bank[:, :], scalar=bias_col, in1=xbuf, op0=ALU.add, op1=ALU.add)
                    P.vS.inc(st1)
                    P.Y.wait_ge(P.vS.h, P.vS.n)
                    P.dmaY(xres3(f)[:, ts0 + qb * 512: ts0 + qb * 512 + 512], xout)
                    if lidx == 2 and "tok" not in L1_SKIP:
                        asmf = [q_sb[:, :].bitcast(F32), k_sb[:, :].bitcast(F32),
                                vfm_sb[:, :].bitcast(F32), p_sb[:, 0:2048].bitcast(F32)]
                        asmb3 = ln_sb[:, 0:4096].rearrange("p (s n) -> p s n", s=4)
                        P.T.wait_ge(P.vS.h, P.vS.n)
                        for sub in range(4):
                            tpm = P.T.transpose(P.pb[2][:, 0:128], xout[:, sub * 128:sub * 128 + 128], idf_sb[:, :])
                            P.tS.inc(tpm)
                            P.V.wait_ge(P.tS.h, P.tS.n)
                            if f == 0:
                                P.V.wait_ge(P.qY.h, P.qY.n)
                                P.V.wait_ge(P.sS.h, P.sS.n)
                            cv = P.V.tensor_copy(asmf[sub][:, f * 128:f * 128 + 128], P.pb[2][:, 0:128])
                            P.vS.inc(cv)
                            P.T.wait_ge(P.vS.h, P.vS.n)
                        if f == NCH - 1:
                            rbase = (ts0 + qb * 512) // 128
                            P.Y.wait_ge(P.vS.h, P.vS.n)
                            for sub in range(4):
                                P.dmaY(d_y.rearrange("(r p) c -> r p c", p=128)[rbase + sub], asmf[sub][:, 0:1024])

        def attn_layer(lidx):
            causal = lidx == 2
            d_w = d_wl0 if lidx == 0 else d_wl1
            d_wo = d_wo0 if lidx == 0 else d_wo1
            wq = warena[:, 0:24576].rearrange("p (c n) -> p c n", c=8)
            wo = warena[:, 24576:32768].rearrange("p (c n) -> p c n", c=8)
            for ch in range(NCH):
                P.dmaG(wq[:, ch, :], d_w.rearrange("(c p) n -> c p n", p=128)[ch])
                P.dmaG(wo[:, ch, :], d_wo.rearrange("(c p) n -> c p n", p=128)[ch])
            if lidx == 0:
                P.dmaG(sc_sb[:, 0:24], d_bl0)
                P.dmaG(sc_sb[:, 24:32], d_bo0)
            else:
                z = P.V.memset(sc_sb[:, 0:32], 0.0)
                P.vS.inc(z)
            lng3 = ent_sb[:, 0:512].rearrange("p (a b) -> p a b", b=16)
            if causal and "gate" not in L1_SKIP:
                l1_trig_prep()
                P.barrier()
                for ch in range(NCH):
                    mmk = P.V.tensor_scalar(out=wo[:, ch, :], in0=wo[:, ch, :], scalar1=sc_sb[:, 40 + ch:41 + ch], scalar2=None, op0=ALU.mult)
                    P.vS.inc(mmk)
                P.barrier()
            if causal and "gate" in L1_SKIP:
                z = P.V.memset(ent_sb[:, 0:512], 0.0)
                P.vS.inc(z)
                P.barrier()
            for s in range(2):
                ts0 = s * SEQ
                with nc.named_scope(f"ln{lidx}_{s}"):
                    ln_stats(ts0)
                    P.barrier()
                    ln3 = ln_sb[:, :].rearrange("p (c n) -> p c n", c=8)
                    ln_apply(ts0, ts0, SEQ, ln3)
                    P.barrier()
                if causal and "gate" not in L1_SKIP:
                    with nc.named_scope(f"gates{s}"):
                        l1_phase_gates(s, ln3, lng3)
                        P.barrier()
                ysb = act_sb[:, :].rearrange("p (c n) -> p c n", c=8)
                with nc.named_scope(f"attn{lidx}_{s}"):
                    for pr in range(8):
                        qkv_pair(lidx, pr, ln3, wq)
                        v_transpose(pr)
                        for qb in range(4):
                            scores_av(lidx, s, pr, qb, lng3)
                            divide_store(pr, qb, ysb)
                with nc.named_scope(f"oproj{lidx}_{s}"):
                    outproj_residual(lidx, s, ysb, wo)
                    P.barrier()

        def l1_trig_prep():
            PI = 3.14159265358979
            row32 = ent_sb[0:1, 512:544]
            tmpr = ent_sb[0:1, 544:576]
            carr = ent_sb[0:1, 576:592]
            bwr = ent_sb[0:1, 592:608]
            ibwr = ent_sb[0:1, 544:560]
            cosc2 = ent_sb[0:1, 560:576]
            sinc2 = ent_sb[0:1, 576:592]
            ehalf_bf = warena[0:16, 32768:32896]
            sel_bf = warena[0:16, 32896:32904]
            m2_bf = warena[0:16, 32904:32912]
            mk_bf = warena[0:16, 32912:32913]
            P.dmaG(carr, d_carrier_r)
            P.dmaG(bwr, d_bw_r)
            P.dmaG(sc_sb[0:16, 48:49], d_lock)
            P.dmaG(sc_sb[0:16, 50:51], d_bw)
            P.dmaG(mk_u8[0:16, 0:1], d_amask)
            P.dmaG(ehalf_bf, d_ehalf)
            P.dmaG(sel_bf, d_sel168)
            P.barrier()
            vS_(P.V.tensor_copy(row32[0:1, 16:32], carr))
            vS_(P.V.tensor_scalar(out=row32[0:1, 0:16], in0=carr, scalar1=PI / 2, scalar2=None, op0=ALU.add))
            for _ in range(10):
                vS_(P.V.tensor_scalar(out=tmpr, in0=row32, scalar1=PI, scalar2=None, op0=ALU.is_gt))
                vS_(P.V.scalar_tensor_tensor(out=row32, in0=tmpr, scalar=-2 * PI, in1=row32, op0=ALU.mult, op1=ALU.add))
                vS_(P.V.tensor_scalar(out=tmpr, in0=row32, scalar1=-PI, scalar2=None, op0=ALU.is_lt))
                vS_(P.V.scalar_tensor_tensor(out=row32, in0=tmpr, scalar=2 * PI, in1=row32, op0=ALU.mult, op1=ALU.add))
            P.S.wait_ge(P.vS.h, P.vS.n)
            sn = P.S.activation(out=row32, in_=row32, func=AF.Sin)
            P.sS.inc(sn)
            P.V.wait_ge(P.sS.h, P.sS.n)
            vS_(P.V.reciprocal(out=ibwr, in_=bwr))
            vS_(P.V.tensor_tensor(out=cosc2, in0=row32[0:1, 0:16], in1=ibwr, op=ALU.mult))
            vS_(P.V.tensor_tensor(out=sinc2, in0=row32[0:1, 16:32], in1=ibwr, op=ALU.mult))
            vS_(P.V.tensor_scalar(out=sc_sb[0:16, 49:50], in0=sc_sb[0:16, 48:49], scalar1=-1.0, scalar2=1.0 + 1e-6, op0=ALU.mult, op1=ALU.add))
            vS_(P.V.reciprocal(out=sc_sb[0:16, 50:51], in_=sc_sb[0:16, 50:51]))
            vS_(P.V.tensor_scalar(out=sc_sb[0:16, 50:51], in0=sc_sb[0:16, 50:51], scalar1=-1.0, scalar2=None, op0=ALU.mult))
            vS_(P.V.tensor_copy(mk_bf, mk_u8[0:16, 0:1]))
            vS_(P.V.tensor_tensor(out=m2_bf, in0=mk_bf.broadcast_to((16, 8)), in1=sel_bf, op=ALU.mult))
            P.T.wait_ge(P.vS.h, P.vS.n)
            mk1 = P.T.matmul(P.pb[3][:, 0:8], ehalf_bf, m2_bf, start=True, stop=True)
            P.tS.inc(mk1)
            P.V.wait_ge(P.tS.h, P.tS.n)
            vS_(P.V.tensor_copy(sc_sb[:, 40:48], P.pb[3][:, 0:8]))

        def l1_phase_gates(s, ln3, lng3):
            ts0 = s * SEQ
            wph_sb = warena[:, 32920:32936].rearrange("p (c n) -> p c n", c=8)
            for ch in range(NCH):
                P.dmaG(wph_sb[:, ch, :], d_wph.rearrange("(c p) n -> c p n", p=128)[ch])
            P.barrier(dma=True)
            ph1 = st_sb[0:1, 0:SEQ]
            ph0 = st_sb[0:1, SEQ:2 * SEQ]
            n2 = st_sb[0:1, 2 * SEQ:3 * SEQ]
            rin = st_sb[0:1, 3 * SEQ:4 * SEQ]
            for b in range(4):
                mm = None
                for ch in range(NCH):
                    mm = P.T.matmul(P.pb[1][0:1, :], wph_sb[:, ch, 1:2], ln3[:, ch, b * 512:b * 512 + 512], start=(ch == 0), stop=(ch == NCH - 1))
                for ch in range(NCH):
                    mm = P.T.matmul(P.pb[2][0:1, :], wph_sb[:, ch, 0:1], ln3[:, ch, b * 512:b * 512 + 512], start=(ch == 0), stop=(ch == NCH - 1))
                P.tS.inc(mm)
                P.V.wait_ge(P.tS.h, P.tS.n)
                vS_(P.V.tensor_copy(ph1[0:1, b * 512:b * 512 + 512], P.pb[1][0:1, :]))
                vS_(P.V.tensor_copy(ph0[0:1, b * 512:b * 512 + 512], P.pb[2][0:1, :]))
                P.T.wait_ge(P.vS.h, P.vS.n)
            vS_(P.V.tensor_tensor(out=n2, in0=ph0, in1=ph0, op=ALU.mult))
            vS_(P.V.tensor_tensor(out=rin, in0=ph1, in1=ph1, op=ALU.mult))
            vS_(P.V.tensor_tensor(out=n2, in0=n2, in1=rin, op=ALU.add))
            P.S.wait_ge(P.vS.h, P.vS.n)
            sq0 = P.S.activation(out=n2, in_=n2, func=AF.Ln)
            P.sS.inc(sq0)
            P.S.wait_ge(P.sS.h, P.sS.n)
            sq = P.S.activation(out=rin, in_=n2, func=AF.Exp, scale=-0.5)
            P.sS.inc(sq)
            P.V.wait_ge(P.sS.h, P.sS.n)
            vS_(P.V.tensor_tensor(out=ph1, in0=ph1, in1=rin, op=ALU.mult))
            vS_(P.V.tensor_tensor(out=ph0, in0=ph0, in1=rin, op=ALU.mult))
            cosc2 = ent_sb[0:1, 560:576]
            sinc2 = ent_sb[0:1, 576:592]
            lngr = xr_sb[0:16, :]
            for b in range(4):
                P.T.wait_ge(P.sS.h, P.sS.n)
                P.T.wait_ge(P.vS.h, P.vS.n)
                P.T.matmul(P.pb[1][0:16, :], cosc2, ph1[0:1, b * 512:b * 512 + 512], start=True, stop=False)
                mm = P.T.matmul(P.pb[1][0:16, :], sinc2, ph0[0:1, b * 512:b * 512 + 512], start=False, stop=True)
                P.tS.inc(mm)
                P.S.wait_ge(P.tS.h, P.tS.n)
                we = P.S.activation(out=ln_extra[0:16, :], in_=P.pb[1][0:16, :], func=AF.Exp, scale=1.0, bias=sc_sb[0:16, 50:51])
                P.sS.inc(we)
                P.V.wait_ge(P.sS.h, P.sS.n)
                vS_(P.V.tensor_scalar(out=ln_extra[0:16, :], in0=ln_extra[0:16, :], scalar1=sc_sb[0:16, 48:49], scalar2=sc_sb[0:16, 49:50], op0=ALU.mult, op1=ALU.add))
                P.S.wait_ge(P.vS.h, P.vS.n)
                lg = P.S.activation(out=lngr[:, b * 512:b * 512 + 512], in_=ln_extra[0:16, :], func=AF.Ln)
                P.sS.inc(lg)
                P.T.wait_ge(P.sS.h, P.sS.n)
            for kt in range(16):
                tpm = P.T.transpose(P.pb[2][:, 0:16], lngr[:, kt * 128:kt * 128 + 128], idf_sb[0:16, 0:16])
                P.tS.inc(tpm)
                P.V.wait_ge(P.tS.h, P.tS.n)
                vS_(P.V.tensor_copy(lng3[:, s * 16 + kt, :], P.pb[2][:, 0:16]))
                P.T.wait_ge(P.vS.h, P.vS.n)

        def emit_sort16_desc(vin, bufA, bufB):
            def pat(buf, d, e, k, j):
                A = 16 // (2 * k) if k < 16 else 1
                m = k // (2 * j)
                base = d * k + e * j
                v = buf.rearrange("p (g x) -> p g x", x=16)
                flat = buf
                if k < 16:
                    r = flat.rearrange("p (g A d m e c) -> p (g A) d m e c", g=4, A=A, d=2, m=m, e=2, c=j)
                    return r[:, :, d, :, e, :]
                else:
                    r = flat.rearrange("p (g m e c) -> p g m e c", g=4, m=m, e=2, c=j)
                    return r[:, :, :, e, :]
            src, dst = vin, (bufB if vin is bufA else bufA)
            for k in [2, 4, 8, 16]:
                j = k // 2
                while j >= 1:
                    if k < 16:
                        for d, first_op in ((0, ALU.max), (1, ALU.min)):
                            lo_s, hi_s = pat(src, d, 0, k, j), pat(src, d, 1, k, j)
                            lo_d, hi_d = pat(dst, d, 0, k, j), pat(dst, d, 1, k, j)
                            sec = ALU.min if first_op == ALU.max else ALU.max
                            vS_(P.V.tensor_tensor(out=lo_d, in0=lo_s, in1=hi_s, op=first_op))
                            vS_(P.V.tensor_tensor(out=hi_d, in0=lo_s, in1=hi_s, op=sec))
                    else:
                        lo_s, hi_s = pat(src, 0, 0, k, j), pat(src, 0, 1, k, j)
                        lo_d, hi_d = pat(dst, 0, 0, k, j), pat(dst, 0, 1, k, j)
                        vS_(P.V.tensor_tensor(out=lo_d, in0=lo_s, in1=hi_s, op=ALU.max))
                        vS_(P.V.tensor_tensor(out=hi_d, in0=lo_s, in1=hi_s, op=ALU.min))
                    src, dst = dst, (bufA if dst is bufB else bufB)
                    j //= 2
            return src

        def emit_cumsum16(buf, tmp):
            src, dst = buf, tmp
            for off in (1, 2, 4, 8):
                s3 = src.rearrange("p (g c) -> p g c", c=16)
                d3 = dst.rearrange("p (g c) -> p g c", c=16)
                vS_(P.V.tensor_copy(d3[:, :, 0:off], s3[:, :, 0:off]))
                vS_(P.V.tensor_tensor(out=d3[:, :, off:16], in0=s3[:, :, off:16], in1=s3[:, :, 0:16 - off], op=ALU.add))
                src, dst = dst, src
            return src

        def ffn0_layer():
            w1 = warena[:, 0:4096].rearrange("p (c n) -> p c n", c=8)
            w1t = warena[:, 4096:20480].rearrange("p (c n) -> p c n", c=4)
            w2t = warena[:, 20480:24576].rearrange("p (c n) -> p c n", c=4)
            wpj = warena[:, 24576:24704].rearrange("p (c n) -> p c n", c=8)
            wc1 = warena[:, 24704:25216]
            wc2 = warena[:, 25216:25728]
            w2b = warena[:, 25728:26752].rearrange("p (c n) -> p c n", c=2)
            for ch in range(NCH):
                P.dmaG(w1[:, ch, :], d_win1.rearrange("(c p) n -> c p n", p=128)[ch])
                P.dmaG(wpj[:, ch, :], d_proj.rearrange("(c p) n -> c p n", p=128)[ch])
            for rc in range(4):
                P.dmaG(w1t[:, rc, :], d_wout1t.rearrange("(c p) n -> c p n", p=128)[rc])
                P.dmaG(w2t[:, rc, :], d_wout2t.rearrange("(c p) n -> c p n", p=128)[rc])
            P.dmaG(wc1[0:16, :], d_c1)
            P.dmaG(wc2[0:16, :], d_c2)
            P.dmaG(sc_sb[:, 32:36], d_fb1)
            P.dmaG(sc_sb[:, 36:40], d_fb2)
            it32 = i32_sb[:, 0:64]
            gio = P.G.iota(it32, pattern=[[0, 4], [1, 16]], base=0, channel_multiplier=0)
            P.pS.inc(gio)
            P.V.wait_ge(P.pS.h, P.pS.n)
            iota16f = ent_sb[:, 0:64]
            P.V.tensor_copy(iota16f, it32)
            rk = ent_sb[:, 64:128]
            vS_(P.V.tensor_scalar(out=rk, in0=iota16f, scalar1=1.0, scalar2=None, op0=ALU.add))
            rkv = vS_(P.V.reciprocal(out=rk, in_=rk))
            ln3f = ln_sb[:, 0:4096].rearrange("p (c n) -> p c n", n=512)
            gflat = ln_sb[:, 4096:12288].bitcast(F32)
            g3 = gflat.rearrange("p (c n) -> p c n", n=512)
            h3 = act_sb[:, :].rearrange("p (o n) -> p o n", n=512)
            af32 = act_sb[:, 0:2048].bitcast(F32)
            z0 = af32[:, 0:64]
            sA = af32[:, 64:128]
            sB = af32[:, 128:192]
            csb = af32[:, 192:256]
            cs2 = af32[:, 256:320]
            tm1 = af32[:, 320:384]
            tm2 = af32[:, 384:448]
            tau = af32[:, 448:512]
            zin = af32[:, 512:576]
            m16c = sc_sb[:, 48:52]
            rs16 = sc_sb[:, 52:56]
            zmx = sc_sb[:, 56:60]
            sup = sc_sb[:, 60:64]
            P.barrier()
            for s in range(2):
                ts0 = s * SEQ
                ln_stats(ts0)
                P.barrier()
                for blk in range(4):
                    tb = ts0 + blk * 512
                    ln_apply(ts0, tb, 512, ln3f)
                    P.barrier(dma=False)
                    dbg_on = dump and s == 0 and blk == 0

                    def dbg_dump(src_ap, off, n, rows=128):
                        if not dbg_on:
                            return
                        P.V.wait_ge(P.qY.h, P.qY.n)
                        dv = P.V.tensor_copy(rb_sb[0:rows, 0:n], src_ap)
                        P.vS.inc(dv)
                        P.Y.wait_ge(P.vS.h, P.vS.n)
                        P.dmaY(d_dbg[0:rows, off:off + n], rb_sb[0:rows, 0:n])
                    dbg_dump(ln3f[:, 0, :], 0, 512)
                    mm = None
                    for ch in range(NCH):
                        mm = P.T.matmul(P.pb[0][0:16, :], wpj[:, ch, :], ln3f[:, ch, :], start=(ch == 0), stop=(ch == NCH - 1))
                    P.tS.inc(mm)
                    P.V.wait_ge(P.tS.h, P.tS.n)
                    cva = vS_(P.V.tensor_copy(ln_extra[0:16, :], P.pb[0][0:16, :]))
                    dbg_dump(ln_extra[0:16, :], 512, 512, rows=16)
                    P.T.wait_ge(P.vS.h, P.vS.n)
                    for tt in range(4):
                        t1 = P.T.transpose(P.pb[2][:, 0:16], ln_extra[0:16, tt * 128:tt * 128 + 128], idf_sb[0:16, 0:16])
                        P.tS.inc(t1)
                        P.V.wait_ge(P.tS.h, P.tS.n)
                        cvt = vS_(P.V.tensor_copy(zin.rearrange("p (g c) -> p g c", c=16)[:, tt:tt + 1, :], P.pb[2][:, 0:16]))
                        P.T.wait_ge(P.vS.h, P.vS.n)
                    z3 = zin.rearrange("p (g c) -> p g c", c=16)
                    dbg_dump(zin, 1024, 64)
                    vS_(P.V.tensor_reduce(out=m16c, in_=z3, axis=mybir.AxisListType.X, op=ALU.add))
                    vS_(P.V.tensor_scalar(out=m16c, in0=m16c, scalar1=1.0 / 16, scalar2=None, op0=ALU.mult))
                    dbg_dump(m16c, 1408, 4)
                    dbg_dump(m16c, 1424, 4)
                    dbg_dump(zin, 1472, 64)
                    if dbg_on:
                        vS_(P.V.tensor_reduce(out=tm2[:, 8:12], in_=z3, axis=mybir.AxisListType.X, op=ALU.add))
                        vS_(P.V.tensor_scalar(out=tm2[:, 8:12], in0=tm2[:, 8:12], scalar1=1.0 / 16, scalar2=None, op0=ALU.mult))
                        dbg_dump(tm2[:, 8:12], 1432, 4)
                    vS_(P.V.tensor_tensor(out=tm1, in0=zin, in1=zin, op=ALU.mult))
                    vS_(P.V.tensor_reduce(out=rs16, in_=tm1.rearrange("p (g c) -> p g c", c=16), axis=mybir.AxisListType.X, op=ALU.add))
                    vS_(P.V.tensor_scalar(out=rs16, in0=rs16, scalar1=1.0 / 16, scalar2=None, op0=ALU.mult))
                    vS_(P.V.tensor_tensor(out=tm2[:, 0:4], in0=m16c, in1=m16c, op=ALU.mult))
                    vS_(P.V.tensor_tensor(out=rs16, in0=rs16, in1=tm2[:, 0:4], op=ALU.subtract))
                    v1 = vS_(P.V.tensor_scalar(out=rs16, in0=rs16, scalar1=1e-5, scalar2=None, op0=ALU.add))
                    P.S.wait_ge(P.vS.h, P.vS.n)
                    s1 = P.S.activation(out=rs16, in_=rs16, func=AF.Sqrt)
                    P.sS.inc(s1)
                    P.V.wait_ge(P.sS.h, P.sS.n)
                    vS_(P.V.reciprocal(out=rs16, in_=rs16))
                    dbg_dump(rs16, 1416, 4)
                    m3 = m16c.rearrange("p (g o) -> p g o", o=1).broadcast_to((128, 4, 16))
                    r3 = rs16.rearrange("p (g o) -> p g o", o=1).broadcast_to((128, 4, 16))
                    vS_(P.V.tensor_tensor(out=z3, in0=z3, in1=m3, op=ALU.subtract))
                    vS_(P.V.tensor_tensor(out=z3, in0=z3, in1=r3, op=ALU.mult))
                    vS_(P.V.tensor_scalar(out=zin, in0=zin, scalar1=0.5, scalar2=None, op0=ALU.mult))
                    vS_(P.V.tensor_reduce(out=zmx, in_=z3, axis=mybir.AxisListType.X, op=ALU.max))
                    zx3 = zmx.rearrange("p (g o) -> p g o", o=1).broadcast_to((128, 4, 16))
                    vS_(P.V.tensor_tensor(out=z0.rearrange("p (g c) -> p g c", c=16), in0=z3, in1=zx3, op=ALU.subtract))
                    dbg_dump(zin, 1088, 64)
                    dbg_dump(z0, 1152, 64)
                    vS_(P.V.tensor_copy(sA, z0))
                    sorted_buf = emit_sort16_desc(sA, sA, sB)
                    vS_(P.V.tensor_copy(csb, sorted_buf))
                    csr = emit_cumsum16(csb, tm2)
                    vS_(P.V.tensor_tensor(out=cs2, in0=sorted_buf, in1=sorted_buf, op=ALU.mult))
                    cs2r = emit_cumsum16(cs2, tm2)
                    vS_(P.V.tensor_tensor(out=csr, in0=csr, in1=rk, op=ALU.mult))
                    vS_(P.V.tensor_tensor(out=cs2r, in0=cs2r, in1=rk, op=ALU.mult))
                    csb2 = csr
                    vS_(P.V.tensor_tensor(out=tm1, in0=csb2, in1=csb2, op=ALU.mult))
                    vS_(P.V.tensor_tensor(out=tm1, in0=cs2r, in1=tm1, op=ALU.subtract))
                    vS_(P.V.tensor_scalar(out=tm2, in0=iota16f, scalar1=1.0, scalar2=None, op0=ALU.add))
                    vS_(P.V.tensor_tensor(out=tm1, in0=tm1, in1=tm2, op=ALU.mult))
                    vS_(P.V.tensor_scalar(out=tm1, in0=tm1, scalar1=-1.0, scalar2=1.0, op0=ALU.mult, op1=ALU.add))
                    vS_(P.V.tensor_tensor(out=tm1, in0=tm1, in1=rk, op=ALU.mult))
                    v2 = vS_(P.V.tensor_scalar(out=tm1, in0=tm1, scalar1=0.0, scalar2=None, op0=ALU.max))
                    P.S.wait_ge(P.vS.h, P.vS.n)
                    s2 = P.S.activation(out=tm1, in_=tm1, func=AF.Sqrt)
                    P.sS.inc(s2)
                    P.V.wait_ge(P.sS.h, P.sS.n)
                    vS_(P.V.tensor_tensor(out=tau, in0=csb2, in1=tm1, op=ALU.subtract))
                    vS_(P.V.tensor_tensor(out=tm1, in0=tau, in1=sorted_buf, op=ALU.is_le))
                    vS_(P.V.tensor_reduce(out=sup, in_=tm1.rearrange("p (g c) -> p g c", c=16), axis=mybir.AxisListType.X, op=ALU.add))
                    vS_(P.V.tensor_scalar(out=sup, in0=sup, scalar1=-1.0, scalar2=None, op0=ALU.add))
                    s3b = sup.rearrange("p (g o) -> p g o", o=1).broadcast_to((128, 4, 16))
                    vS_(P.V.tensor_tensor(out=tm1, in0=iota16f.rearrange("p (g c) -> p g c", c=16), in1=s3b, op=ALU.is_equal))
                    vS_(P.V.tensor_tensor(out=tm1, in0=tau, in1=tm1, op=ALU.mult))
                    vS_(P.V.tensor_reduce(out=zmx, in_=tm1.rearrange("p (g c) -> p g c", c=16), axis=mybir.AxisListType.X, op=ALU.add))
                    tx3 = zmx.rearrange("p (g o) -> p g o", o=1).broadcast_to((128, 4, 16))
                    vS_(P.V.tensor_tensor(out=z0.rearrange("p (g c) -> p g c", c=16), in0=z0.rearrange("p (g c) -> p g c", c=16), in1=tx3, op=ALU.subtract))
                    v4 = vS_(P.V.tensor_scalar(out=z0, in0=z0, scalar1=0.0, scalar2=None, op0=ALU.max))
                    P.S.wait_ge(P.vS.h, P.vS.n)
                    s4 = P.S.activation(out=z0, in_=z0, func=AF.Square)
                    P.sS.inc(s4)
                    if dbg_on:
                        P.V.wait_ge(P.sS.h, P.sS.n)
                    dbg_dump(z0, 1216, 64)
                    dbg_dump(sorted_buf, 1280, 64)
                    dbg_dump(tau, 1344, 64)
                    a_fm = vkm_sb[0:16, 0:512]
                    P.T.wait_ge(P.sS.h, P.sS.n)
                    for tt in range(4):
                        t2 = P.T.transpose(P.pb[2][0:16, 0:128], z0[:, tt * 16:tt * 16 + 16], idf_sb[:, :])
                        P.tS.inc(t2)
                        P.V.wait_ge(P.tS.h, P.tS.n)
                        cvb = vS_(P.V.tensor_copy(a_fm[:, tt * 128:tt * 128 + 128], P.pb[2][0:16, 0:128]))
                        P.T.wait_ge(P.vS.h, P.vS.n)
                    for half, wc in ((0, wc1), (1, wc2)):
                        for rc in range(4):
                            g1m = P.T.matmul(P.pb[3][:, :], wc[0:16, rc * 128:rc * 128 + 128], a_fm, start=True, stop=True)
                            P.tS.inc(g1m)
                            P.V.wait_ge(P.tS.h, P.tS.n)
                            gv = vS_(P.V.tensor_copy(g3[:, half * 4 + rc, :], P.pb[3][:, :]))
                            P.T.wait_ge(P.vS.h, P.vS.n)
                    dbg_dump(g3[:, 0, :], 1536, 512)
                    dbg_dump(a_fm, 2048, 512, rows=16)
                    z1g = q_sb[:, :].rearrange("p (c n) -> p c n", n=512)
                    for rc in range(4):
                        bank = P.pb[0 + (rc % 2)]
                        P.T.wait_ge(P.vS.h, P.vS.n)
                        mm = None
                        for ch in range(NCH):
                            mm = P.T.matmul(bank[:, :], w1[:, ch, rc * 128:rc * 128 + 128], ln3f[:, ch, :], start=(ch == 0), stop=(ch == NCH - 1))
                        P.tS.inc(mm)
                        P.V.wait_ge(P.tS.h, P.tS.n)
                        sv = P.V.scalar_tensor_tensor(out=z1g[:, rc, :], in0=bank[:, :], scalar=sc_sb[:, 32 + rc:33 + rc], in1=g3[:, rc, :], op0=ALU.add, op1=ALU.mult)
                        P.vS.inc(sv)
                    dbg_dump(z1g[:, 0, :], 2560, 512)
                    P.T.wait_ge(P.vS.h, P.vS.n)
                    for oc in range(32):
                        bank = P.pb[0 + (oc % 2)]
                        P.T.wait_ge(P.sS.h, P.sS.n)
                        mm = None
                        for rc in range(4):
                            mm = P.T.matmul(bank[:, :], w1t[:, rc, oc * 128:oc * 128 + 128], z1g[:, rc, :], start=(rc == 0), stop=(rc == 3))
                        P.tS.inc(mm)
                        P.S.wait_ge(P.tS.h, P.tS.n)
                        ge = P.S.activation(out=h3[:, oc, :], in_=bank[:, :], func=AF.Gelu)
                        P.sS.inc(ge)
                    if dbg_on:
                        P.V.wait_ge(P.sS.h, P.sS.n)
                    dbg_dump(h3[:, 0, :], 3072, 512)
                    z2g = k_sb[:, :].rearrange("p (c n) -> p c n", n=512)
                    P.T.wait_ge(P.sS.h, P.sS.n)
                    loaded = []
                    mmoc = {}
                    for oc in range(2):
                        P.dmaG(w2b[:, oc % 2, :], d_win2.rearrange("(c p) n -> c p n", p=128)[oc])
                        loaded.append(P.qG.n)
                    for oc in range(32):
                        wslot = w2b[:, oc % 2, :]
                        P.T.wait_ge(P.qG.h, loaded[oc])
                        mm = None
                        for rc in range(4):
                            mm = P.T.matmul(P.pb[2 + rc][:, :], wslot[:, rc * 128:rc * 128 + 128], h3[:, oc, :], start=(oc == 0), stop=(oc == 31))
                        mmoc[oc] = P.tS.inc(mm)
                        if oc + 2 < 32:
                            P.G.wait_ge(P.tS.h, mmoc[oc])
                            P.dmaG(w2b[:, oc % 2, :], d_win2.rearrange("(c p) n -> c p n", p=128)[oc + 2])
                            loaded.append(P.qG.n)
                    P.V.wait_ge(P.tS.h, P.tS.n)
                    for rc in range(4):
                        sv = P.V.scalar_tensor_tensor(out=z2g[:, rc, :], in0=P.pb[2 + rc][:, :], scalar=sc_sb[:, 36 + rc:37 + rc], in1=g3[:, 4 + rc, :], op0=ALU.add, op1=ALU.mult)
                        P.vS.inc(sv)
                    dbg_dump(z2g[:, 0, :], 3584, 512)
                    P.T.wait_ge(P.vS.h, P.vS.n)
                    for f in range(NCH):
                        bank = P.pb[0 + (f % 2)]
                        P.T.wait_ge(P.vS.h, P.vS.n)
                        mm = None
                        for rc in range(4):
                            mm = P.T.matmul(bank[:, :], w2t[:, rc, f * 128:f * 128 + 128], z2g[:, rc, :], start=(rc == 0), stop=(rc == 3))
                        P.tS.inc(mm)
                        xbuf = xr_sb[:, (f % 2) * 512:(f % 2) * 512 + 512]
                        P.Y.wait_ge(P.vS.h, P.vS.n)
                        P.dmaY(xbuf, xres3(f)[:, tb:tb + 512])
                        P.V.wait_ge(P.tS.h, P.tS.n)
                        P.V.wait_ge(P.qY.h, P.qY.n)
                        xout = xr_sb[:, (2 + f % 2) * 512:(2 + f % 2) * 512 + 512]
                        st1 = P.V.scalar_tensor_tensor(out=xout, in0=bank[:, :], scalar=0.0, in1=xbuf, op0=ALU.add, op1=ALU.add)
                        P.vS.inc(st1)
                        P.Y.wait_ge(P.vS.h, P.vS.n)
                        P.dmaY(xres3(f)[:, tb:tb + 512], xout)
                    P.barrier(dma=False)
                P.barrier()


        def moe_layer():
            TRASH = NEXP * C_CAP + 2
            actf = act_sb[:, :].bitcast(F32)
            lnf = ln_sb[:, :].bitcast(F32)
            ln3f = ln_sb[:, 0:4096].rearrange("p (c n) -> p c n", n=512)
            sm8 = actf[0:8, 0:4096]
            mk8 = actf[0:8, 4096:8192]
            pos8 = lnf[0:8, 2048:6144]
            sel8 = lnf[0:8, 6144:8192]
            wrb = vkm_sb[:, 1024:1152].bitcast(F32).rearrange("p (c n) -> p c n", c=8)
            ln4f3 = lnf[:, 4096:8192].rearrange("p (c n) -> p c n", n=512)
            dest_i = i32_sb[:, 0:256].rearrange("p (t e) -> p t e", e=8)
            x3s = warena[:, 0:36864].rearrange("p (t c) -> p t c", c=1152)
            for ch in range(NCH):
                P.dmaY(wrb[:, ch, :], d_wr.rearrange("(c p) n -> c p n", p=128)[ch])
            P.dmaG(sq_sb[0:8, 136:137] if False else sc_sb[0:8, 62:63], d_br)
            P.dmaG(i32_sb[0:32, 256:257], d_bisrows)
            P.dmaG(sc_sb[0:8, 63:64], d_ecol)
            for s in range(2):
              with nc.named_scope(f"router{s}"):
                ts0 = s * SEQ
                ln_stats(ts0)
                P.barrier()
                for blk in range(4):
                    tb = ts0 + blk * 512
                    ln_apply(ts0, tb, 512, ln4f3)
                    P.barrier()
                    mm = None
                    for ch in range(NCH):
                        mm = P.T.matmul(P.pb[0][0:8, :], wrb[:, ch, :], ln4f3[:, ch, :], start=(ch == 0), stop=(ch == NCH - 1))
                    P.tS.inc(mm)
                    P.S.wait_ge(P.tS.h, P.tS.n)
                    ex = P.S.activation(out=sm8[:, tb:tb + 512], in_=P.pb[0][0:8, :], func=AF.Exp, scale=1.0, bias=sc_sb[0:8, 62:63])
                    P.sS.inc(ex)
                    P.T.wait_ge(P.sS.h, P.sS.n)
                    asmb3r = ln_sb[:, 0:4096].rearrange("p (s n) -> p s n", s=4)
                    for f in range(NCH):
                        for sub in range(4):
                            tpl = P.T.transpose(P.pb[2 + (sub % 2)][:, 0:128], ln4f3[:, f, sub * 128:sub * 128 + 128], idf_sb[:, :])
                            P.tS.inc(tpl)
                            P.S.wait_ge(P.tS.h, P.tS.n)
                            if f == 0:
                                P.S.wait_ge(P.qY.h, P.qY.n)
                            cbl = P.S.activation(out=asmb3r[:, sub, f * 128:f * 128 + 128], in_=P.pb[2 + (sub % 2)][:, 0:128], func=AF.Identity)
                            P.sS.inc(cbl)
                            P.T.wait_ge(P.sS.h, P.sS.n)
                    P.Y.wait_ge(P.sS.h, P.sS.n)
                    for sub in range(4):
                        P.dmaY(d_x3bf.rearrange("(r p) c -> r p c", p=128)[(tb // 128) + sub], asmb3r[:, sub, :])
                    P.barrier()
            P.barrier()
            for blk in range(8):
                t0 = blk * 512
                m1 = P.T.matmul(P.pb[1][0:1, :], on_sb[0:8, 0:1], sm8[:, t0:t0 + 512], start=True, stop=True)
                P.tS.inc(m1)
                P.V.wait_ge(P.tS.h, P.tS.n)
                vS_(P.V.reciprocal(out=st_sb[0:1, t0:t0 + 512], in_=P.pb[1][0:1, :]))
                P.T.wait_ge(P.vS.h, P.vS.n)
                m2 = P.T.matmul(P.pb[2][0:8, :], onesr_sb[0:1, 0:8], st_sb[0:1, t0:t0 + 512], start=True, stop=True)
                P.tS.inc(m2)
                P.V.wait_ge(P.tS.h, P.tS.n)
                vS_(P.V.tensor_tensor(out=sm8[:, t0:t0 + 512], in0=sm8[:, t0:t0 + 512], in1=P.pb[2][0:8, :], op=ALU.mult))
                P.T.wait_ge(P.vS.h, P.vS.n)
            if MOE_STOP <= 1:
                return
            P.Y.wait_ge(P.vS.h, P.vS.n)
            P.dmaY(d_smi.ap(), sm8)
            P.G.wait_ge(P.qY.h, P.qY.n)
            cc1 = P.G.collective_compute(
                "AllGather", ALU.bypass, replica_groups=[list(range(NCORE))],
                ins=[d_smi.ap().opt()], outs=[d_smo.ap().opt()])
            P.ccS.inc(cc1, 1)
            P.G.wait_ge(P.ccS.h, P.ccS.n)
            bis = xr_sb[0:32, 0:1024]
            gi = P.G.indirect_dma_start(
                out=bis, out_offset=None,
                in_=d_smo.ap().rearrange("a (q u) -> (a q) u", u=1024),
                in_offset=bass.IndirectOffsetOnAxis(ap=i32_sb[0:32, 256:257], axis=0))
            P.qG.inc(gi, 16)
            P.barrier()
            if MOE_STOP <= 2:
                return
            bis_scope = nc.enter_named_scope("bisect", False)
            mskb = xr_sb[32:64, 0:1024]
            cells = sq_sb[0:1, 128:136]
            lo_c = sq_sb[0:1, 128:129]
            hi_c = sq_sb[0:1, 129:130]
            mid_c = sq_sb[0:1, 130:131]
            ge_c = sq_sb[0:1, 131:132]
            tot_c = sq_sb[0:1, 132:133]
            vS_(P.V.memset(lo_c, 0.0))
            vS_(P.V.memset(hi_c, 1.0))
            vS_(P.V.memset(mid_c, 0.5))
            tau_col = sq_sb[0:32, 32:33]
            for it in range(36):
                vS_(P.V.tensor_scalar(out=sq_sb[0:1, 0:32], in0=onesr_sb[0:1, 0:32], scalar1=mid_c, scalar2=None, op0=ALU.mult))
                vS_(P.V.transpose(out=sq_sb[0:32, 32:64], in_=sq_sb[0:32, 0:32]))
                vS_(P.V.tensor_scalar(out=mskb, in0=bis, scalar1=tau_col, scalar2=0.0, op0=ALU.is_ge, op1=ALU.add, accum_out=sq_sb[0:32, 64:65]))
                vS_(P.V.transpose(out=sq_sb[0:32, 96:128], in_=sq_sb[0:32, 64:96]))
                vS_(P.V.tensor_reduce(out=tot_c, in_=sq_sb[0:1, 96:128], axis=mybir.AxisListType.X, op=ALU.add))
                vS_(P.V.tensor_scalar(out=ge_c, in0=tot_c, scalar1=4095.5, scalar2=None, op0=ALU.is_ge))
                vS_(P.V.select(out=lo_c, mask=ge_c.bitcast(I32), on_true=mid_c, on_false=lo_c))
                vS_(P.V.select(out=hi_c, mask=ge_c.bitcast(I32), on_true=hi_c, on_false=mid_c))
                vS_(P.V.tensor_tensor(out=mid_c, in0=lo_c, in1=hi_c, op=ALU.add))
                vS_(P.V.tensor_scalar(out=mid_c, in0=mid_c, scalar1=0.5, scalar2=None, op0=ALU.mult))
            P.Y.wait_ge(P.vS.h, P.vS.n)
            if MOE_STOP <= 3:
                return
            P.dmaY(d_taui.ap(), sq_sb[0:1, 128:144])
            P.G.wait_ge(P.qY.h, P.qY.n)
            cc2 = P.G.collective_compute(
                "AllGather", ALU.bypass, replica_groups=[list(range(NCORE))],
                ins=[d_taui.ap().opt()], outs=[d_tauo.ap().opt()])
            P.ccS.inc(cc2, 1)
            P.G.wait_ge(P.ccS.h, P.ccS.n)
            gtau = P.G.dma_start(out=sq_sb[0:8, 136:137], in_=d_tauo.ap()[:, 0:1], allow_slow_non_contiguous=True)
            P.qG.inc(gtau, 16)
            P.barrier()
            nc.leave_named_scope("bisect", bis_scope[0], False)
            if MOE_STOP <= 4:
                return
            vS_(P.V.tensor_scalar(out=mk8, in0=sm8, scalar1=sq_sb[0:8, 136:137], scalar2=None, op0=ALU.is_ge))
            vS_(P.V.tensor_tensor_scan(out=pos8, data0=on_sb[0:8, 0:1].broadcast_to((8, 4096)), data1=mk8, initial=0.0, op0=ALU.mult, op1=ALU.add))
            for hf in range(2):
                hs = slice(hf * 2048, hf * 2048 + 2048)
                vS_(P.V.tensor_scalar(out=sel8, in0=pos8[:, hs], scalar1=float(C_CAP) + 0.5, scalar2=None, op0=ALU.is_le))
                vS_(P.V.tensor_tensor(out=mk8[:, hs], in0=mk8[:, hs], in1=sel8, op=ALU.mult))
                vS_(P.V.tensor_scalar(out=pos8[:, hs], in0=pos8[:, hs], scalar1=sc_sb[0:8, 63:64], scalar2=None, op0=ALU.add))
                vS_(P.V.tensor_scalar(out=pos8[:, hs], in0=pos8[:, hs], scalar1=-float(TRASH), scalar2=None, op0=ALU.add))
                vS_(P.V.tensor_tensor(out=pos8[:, hs], in0=pos8[:, hs], in1=mk8[:, hs], op=ALU.mult))
                vS_(P.V.tensor_scalar(out=pos8[:, hs], in0=pos8[:, hs], scalar1=float(TRASH), scalar2=None, op0=ALU.add))
            if dump:
                def mdump(src_ap, off, n, rows=128):
                    P.V.wait_ge(P.qY.h, P.qY.n)
                    dv = P.V.tensor_copy(rb_sb[0:rows, 0:n], src_ap)
                    P.vS.inc(dv)
                    P.Y.wait_ge(P.vS.h, P.vS.n)
                    P.dmaY(d_dbg[0:rows, off:off + n], rb_sb[0:rows, 0:n], allow_slow_non_contiguous=True)
                mdump(sm8[:, 0:512], 4096, 512, rows=8)
                mdump(sq_sb[0:8, 136:137], 4608, 1, rows=8)
                mdump(sq_sb[0:1, 128:133], 4612, 5, rows=1)
                mdump(mk8[:, 0:512], 4620, 512, rows=8)
                mdump(pos8[:, 0:512], 5132, 512, rows=8)
            x3l = P.dmaG(x3s[:, :, 0:1024], d_x3bf.rearrange("(t p) c -> p t c", p=128))
            it1 = P.G.iota(i32_sb[:, 0:32], pattern=[[1, 32]], base=0, channel_multiplier=0)
            it2 = P.G.iota(i32_sb[:, 32:64], pattern=[[0, 32]], base=0, channel_multiplier=1)
            it3 = P.G.iota(i32_sb[:, 64:65], pattern=[[1, 1]], base=0, channel_multiplier=1)
            P.pS.inc(it3)
            P.V.wait_ge(P.pS.h, P.pS.n)
            vS_(P.V.tensor_copy(sq_sb[:, 137:169], i32_sb[:, 0:32]))
            vS_(P.V.tensor_copy(sq_sb[:, 169:201], i32_sb[:, 32:64]))
            vS_(P.V.tensor_copy(sq_sb[:, 201:202], i32_sb[:, 64:65]))
            P.barrier()
            for t in range(32):
                tp1 = P.T.transpose(P.pb[3][:, 0:8], pos8[:, t * 128:t * 128 + 128], idf_sb[0:8, 0:8])
                tp2 = P.T.transpose(P.pb[4][:, 0:8], sm8[:, t * 128:t * 128 + 128], idf_sb[0:8, 0:8])
                P.tS.inc(tp2)
                P.V.wait_ge(P.tS.h, P.tS.n)
                vS_(P.V.tensor_copy(dest_i[:, t, :], P.pb[3][:, 0:8]))
                vS_(P.V.tensor_copy(x3s[:, t, 1024:1032], P.pb[4][:, 0:8]))
                P.T.wait_ge(P.vS.h, P.vS.n)
            vS_(P.V.tensor_copy(x3s[:, :, 1032:1033], sq_sb[:, 137:169].rearrange("p (t o) -> p t o", o=1)))
            vS_(P.V.tensor_copy(x3s[:, :, 1033:1034], sq_sb[:, 169:201].rearrange("p (t o) -> p t o", o=1)))
            P.barrier()
            if MOE_STOP <= 5:
                return
            cmp_scope = nc.enter_named_scope("compact", False)
            scat_done = []
            for e in range(NEXP):
                for t in range(32):
                    sc1 = P.G.indirect_dma_start(
                        out=d_comp, out_offset=bass.IndirectOffsetOnAxis(ap=dest_i[:, t, e:e + 1], axis=0),
                        in_=x3s[:, t, :], in_offset=None)
                    P.qG.inc(sc1, 16)
                scat_done.append(P.qG.n)
            scat_done = [P.qG.n] * NEXP
            P.G.wait_ge(P.qG.h, P.qG.n)
            P.barrier()
            nc.leave_named_scope("compact", cmp_scope[0], False)
            if MOE_STOP <= 6:
                P.G.wait_ge(P.qG.h, P.qG.n)
                P.barrier()
                return
            wd3 = warena[:, 0:22528].rearrange("p (fc d) -> p fc d", d=1024)
            wu_db = warena[:, 22528:24576].rearrange("p (b c n) -> p b c n", b=2, c=8)
            gx = act_sb[:, 0:6144].rearrange("p (c n) -> p c n", c=8)
            ch9_sb = vkm_sb[:, 0:768]
            def mid_ap(fc):
                if fc < 21:
                    return ln_sb[:, fc * 768:(fc + 1) * 768]
                return vkm_sb[:, 0:768]
            g_r = st_sb[0:1, 0:768]
            hi_r = st_sb[0:1, 768:1536]
            lo_r = st_sb[0:1, 1536:2304]
            idx_r = st_sb[0:1, 2304:3072]
            gcols = sq_sb[:, 256:264]
            idxcols = i32_sb[:, 260:268]
            iotapf = sq_sb[:, 201:202]
            ohb = p_sb[:, 0:1]
            oh8 = p_sb[:, 1:2]
            oh9 = p_sb[:, 2:3]
            vS_(P.V.tensor_scalar(out=sq_sb[:, 202:203], in0=iotapf, scalar1=8.0, scalar2=None, op0=ALU.is_equal))
            vS_(P.V.tensor_scalar(out=sq_sb[:, 203:204], in0=iotapf, scalar1=9.0, scalar2=None, op0=ALU.is_equal))
            vS_(P.V.tensor_copy(oh8, sq_sb[:, 202:203]))
            vS_(P.V.tensor_copy(oh9, sq_sb[:, 203:204]))
            for e in range(NEXP):
              with nc.named_scope(f"exp{e}"):
                g = e // 2
                P.S.wait_ge(P.sS.h, P.sS.n)
                P.G.wait_ge(P.qG.h, scat_done[e])
                wl = P.dmaG(wd3[:, :, :], d_wdn[e].rearrange("(fc p) d -> p fc d", p=128))
                P.dmaA(sc_sb[:, 32:54], d_bup[g])
                bl = P.dmaA(sc_sb[:, 54:62], d_bdn[e])
                P.Y.wait_ge(P.qG.h, scat_done[e])
                for c9 in range(8):
                    P.dmaY(gx[:, c9, :], d_comp[e * C_CAP:(e + 1) * C_CAP, c9 * 128:c9 * 128 + 128], transpose=True)
                P.dmaY(ch9_sb, d_comp[e * C_CAP:(e + 1) * C_CAP, 1024:1152][:, 0:128], transpose=True)
                vS_(P.V.tensor_scalar(out=sq_sb[:, 204:205], in0=iotapf, scalar1=float(e), scalar2=None, op0=ALU.is_equal))
                vS_(P.V.tensor_copy(ohb, sq_sb[:, 204:205]))
                P.barrier()
                for h2 in range(2):
                    mg = P.T.matmul(P.pb[0][0:1, 0:384], ohb, ch9_sb[:, h2 * 384:h2 * 384 + 384], start=True, stop=True)
                    mh = P.T.matmul(P.pb[1][0:1, 0:384], oh8, ch9_sb[:, h2 * 384:h2 * 384 + 384], start=True, stop=True)
                    ml = P.T.matmul(P.pb[2][0:1, 0:384], oh9, ch9_sb[:, h2 * 384:h2 * 384 + 384], start=True, stop=True)
                    P.tS.inc(ml)
                    P.V.wait_ge(P.tS.h, P.tS.n)
                    vS_(P.V.tensor_copy(g_r[0:1, h2 * 384:h2 * 384 + 384], P.pb[0][0:1, 0:384]))
                    vS_(P.V.tensor_copy(hi_r[0:1, h2 * 384:h2 * 384 + 384], P.pb[1][0:1, 0:384]))
                    vS_(P.V.tensor_copy(lo_r[0:1, h2 * 384:h2 * 384 + 384], P.pb[2][0:1, 0:384]))
                    P.T.wait_ge(P.vS.h, P.vS.n)
                vS_(P.V.scalar_tensor_tensor(out=idx_r, in0=hi_r, scalar=128.0, in1=lo_r, op0=ALU.mult, op1=ALU.add))
                if dump and e == 0:
                    mdump(g_r[0:1, 0:512], 5648, 512, rows=1)
                    mdump(idx_r[0:1, 0:512], 6160, 512, rows=1)
                    mdump(gx[:, 0, 0:512], 6672, 512)
                P.T.wait_ge(P.vS.h, P.vS.n)
                for st6 in range(6):
                    tg = P.T.transpose(P.pb[3][:, 0:1], g_r[0:1, st6 * 128:st6 * 128 + 128], idf_sb[0:1, 0:1])
                    ti = P.T.transpose(P.pb[4][:, 0:1], idx_r[0:1, st6 * 128:st6 * 128 + 128], idf_sb[0:1, 0:1])
                    P.tS.inc(ti)
                    P.V.wait_ge(P.tS.h, P.tS.n)
                    vS_(P.V.tensor_copy(gcols[:, st6:st6 + 1], P.pb[3][:, 0:1]))
                    vS_(P.V.tensor_copy(idxcols[:, st6:st6 + 1], P.pb[4][:, 0:1]))
                    P.T.wait_ge(P.vS.h, P.vS.n)
                wuload = []
                wumm = {}
                for fc in range(2):
                    P.dmaG(wu_db[:, fc % 2, :, :], d_wup[g][:, fc * 128:fc * 128 + 128].rearrange("(c p) n -> p c n", p=128))
                    wuload.append(P.qG.n)
                for fc in range(22):
                    slot = fc % 2
                    P.T.wait_ge(P.qG.h, wuload[fc])
                    for h2 in range(2):
                        bank = P.pb[(fc * 2 + h2) % 2]
                        P.T.wait_ge(P.sS.h, P.sS.n)
                        mm = None
                        for ch in range(NCH):
                            mm = P.T.matmul(bank[:, 0:384], wu_db[:, slot, ch, :], gx[:, ch, h2 * 384:h2 * 384 + 384], start=(ch == 0), stop=(ch == NCH - 1))
                        wumm[fc] = P.tS.inc(mm)
                        P.S.wait_ge(P.tS.h, P.tS.n)
                        sv = P.S.activation(out=mid_ap(fc)[:, h2 * 384:h2 * 384 + 384], in_=bank[:, 0:384], func=AF.Silu, bias=sc_sb[:, 32 + fc:33 + fc], scale=1.0)
                        P.sS.inc(sv)
                        if h2 == 1 and fc + 2 < 22:
                            P.G.wait_ge(P.tS.h, wumm[fc])
                            P.dmaG(wu_db[:, fc % 2, :, :], d_wup[g][:, (fc + 2) * 128:(fc + 2) * 128 + 128].rearrange("(c p) n -> p c n", p=128))
                            wuload.append(P.qG.n)
                        if dump and e == 0 and fc == 0 and h2 == 0:
                            P.V.wait_ge(P.sS.h, P.sS.n)
                            mdump(wu_db[:, slot, 0, :], 7184, 128)
                            mdump(mid_ap(0)[:, 0:384], 7312, 384)
                P.T.wait_ge(P.sS.h, P.sS.n)
                P.T.wait_ge(P.qG.h, P.qG.n)
                P.T.wait_ge(P.qA.h, P.qA.n)
                for stp in range(3):
                    for dc in range(NCH):
                        bank = P.pb[dc % 2]
                        P.T.wait_ge(P.vS.h, P.vS.n)
                        mm = None
                        for fc in range(22):
                            mm = P.T.matmul(bank[:, 0:256], wd3[:, fc, dc * 128:dc * 128 + 128], mid_ap(fc)[:, stp * 256:stp * 256 + 256], start=(fc == 0), stop=(fc == 21))
                        P.tS.inc(mm)
                        P.V.wait_ge(P.tS.h, P.tS.n)
                        vS_(P.V.tensor_scalar(out=rb_sb[:, 0:256], in0=bank[:, 0:256], scalar1=sc_sb[:, 54 + dc:55 + dc], scalar2=None, op0=ALU.add))
                        if dump and e == 0 and stp == 0 and dc == 0:
                            dv = P.V.tensor_copy(ln_extra[:, 0:256], rb_sb[:, 0:256])
                            P.vS.inc(dv)
                            P.Y.wait_ge(P.vS.h, P.vS.n)
                            P.dmaY(d_dbg[:, 7824:8080], ln_extra[:, 0:256])
                        P.T.wait_ge(P.vS.h, P.vS.n)
                        for ti in range(2):
                            tt = P.T.transpose(P.pb[2 + ti][:, 0:128], rb_sb[:, ti * 128:ti * 128 + 128], idf_sb[:, :])
                            P.tS.inc(tt)
                            P.V.wait_ge(P.tS.h, P.tS.n)
                            vS_(P.V.tensor_scalar(out=xr_sb[:, ti * 1024 + dc * 128: ti * 1024 + dc * 128 + 128], in0=P.pb[2 + ti][:, 0:128], scalar1=gcols[:, stp * 2 + ti:stp * 2 + ti + 1], scalar2=None, op0=ALU.mult))
                            P.T.wait_ge(P.vS.h, P.vS.n)
                    P.G.wait_ge(P.vS.h, P.vS.n)
                    for ti in range(2):
                        sa = P.G.indirect_dma_start(
                            out=d_y, out_offset=bass.IndirectOffsetOnAxis(ap=idxcols[:, stp * 2 + ti:stp * 2 + ti + 1], axis=0),
                            in_=xr_sb[:, ti * 1024:ti * 1024 + 1024], in_offset=None,
                            compute_op=ALU.add)
                        P.qG.inc(sa, 16)
                    P.G.wait_ge(P.qG.h, P.qG.n)
                    P.V.wait_ge(P.qG.h, P.qG.n)
                P.barrier()

        if n_sub >= 1:
            with nc.named_scope("L0_attn"):
                attn_layer(0)
        if n_sub >= 2:
            with nc.named_scope("L0_ffn"):
                ffn0_layer()
        if n_sub >= 3:
            with nc.named_scope("L1_attn"):
                attn_layer(2)
        if n_sub >= 4:
            with nc.named_scope("L1_moe"):
                moe_layer()
        if dump:
            P.barrier()
            for ch in range(NCH):
                P.dmaY(d_dump.rearrange("(c p) t -> c p t", p=128)[ch], xres3(ch))

        P.barrier()
    es.close()
    return nc


def _prep_shared(inputs):
    g = {}
    wq0 = inputs["L0_attn_W_qkv"]
    b0 = inputs["L0_attn_b_qkv"]
    cols = []
    bcols = []
    for p in range(8):
        for gsel in range(3):
            for h in (2 * p, 2 * p + 1):
                cols.append(wq0[:, h * 192 + gsel * 64: h * 192 + gsel * 64 + 64])
                bcols.append(b0[h * 192 + gsel * 64: h * 192 + gsel * 64 + 64])
    g["w_l0qkv"] = np.ascontiguousarray(np.concatenate(cols, axis=1))
    bl0 = np.stack([np.concatenate(bcols[i * 2:i * 2 + 2]) for i in range(24)], axis=1)
    g["b_l0qkv"] = np.ascontiguousarray(bl0)
    g["w_l0out"] = inputs["L0_attn_W_out"]
    g["b_l0out"] = np.ascontiguousarray(inputs["L0_attn_b_out"].reshape(8, 128).T)
    wq1 = np.concatenate([inputs["L1_attn_wq"], inputs["L1_attn_wk"], inputs["L1_attn_wv"]], axis=1)
    cols = []
    for p in range(8):
        for gsel in range(3):
            for h in (2 * p, 2 * p + 1):
                cols.append(wq1[:, gsel * 1024 + h * 64: gsel * 1024 + h * 64 + 64])
    g["w_l1qkv"] = np.ascontiguousarray(np.concatenate(cols, axis=1))
    g["w_l1out"] = inputs["L1_attn_wo"]
    g["w_l1ph"] = inputs["L1_attn_w_phase"]
    g["l1_carrier"] = np.ascontiguousarray(inputs["L1_attn_carrier"].reshape(16, 1))
    g["l1_lock"] = np.ascontiguousarray(inputs["L1_attn_lock"].reshape(16, 1))
    g["l1_bw"] = np.ascontiguousarray(inputs["L1_attn_bandwidth"].reshape(16, 1))
    g["l1_carrier_r"] = np.ascontiguousarray(inputs["L1_attn_carrier"].reshape(1, 16))
    g["l1_bw_r"] = np.ascontiguousarray(inputs["L1_attn_bandwidth"].reshape(1, 16))
    g["l1_amask"] = np.ascontiguousarray(inputs["L1_attn_active_mask"].reshape(16, 1)).view(np.uint8)
    g["f0_proj"] = inputs["L0_ffn_proj"]
    g["f0_c1"] = inputs["L0_ffn_C1"]
    g["f0_c2"] = inputs["L0_ffn_C2"]
    g["f0_win1"] = np.ascontiguousarray(inputs["L0_ffn_Win1"][0:1024])
    g["f0_b1"] = np.ascontiguousarray(inputs["L0_ffn_Win1"][1024].reshape(4, 128).T)
    g["f0_wout1t"] = np.ascontiguousarray(inputs["L0_ffn_Wout1"].T)
    g["f0_win2"] = np.ascontiguousarray(inputs["L0_ffn_Win2"][0:4096])
    g["f0_b2"] = np.ascontiguousarray(inputs["L0_ffn_Win2"][4096].reshape(4, 128).T)
    g["f0_wout2t"] = np.ascontiguousarray(inputs["L0_ffn_Wout2"].T)
    g["m_wr"] = inputs["L1_ffn_Wr"]
    g["m_br"] = np.ascontiguousarray(inputs["L1_ffn_br"].reshape(8, 1))
    wup = np.zeros((4, D, FUP), np.float32)
    wup[:, :, :2730] = inputs["L1_ffn_W_up"]
    g["m_wup"] = wup
    bup = np.zeros((4, FUP), np.float32)
    bup[:, :2730] = inputs["L1_ffn_b_up"]
    g["m_bup"] = np.ascontiguousarray(bup.reshape(4, FUP // 128, 128).transpose(0, 2, 1))
    wdn = np.zeros((NEXP, FUP, D), np.float32)
    wdn[:, :2730, :] = inputs["L1_ffn_W_down"]
    g["m_wdn"] = wdn
    g["m_bdn"] = np.ascontiguousarray(inputs["L1_ffn_b_down"].reshape(NEXP, 8, 128).transpose(0, 2, 1))
    g["c_ident"] = np.eye(128, dtype=BF)
    g["c_identf"] = np.eye(128, dtype=np.float32)
    jj, kk = np.meshgrid(np.arange(128), np.arange(128), indexing="ij")
    g["c_lneg"] = ((kk > jj) * -1e30).astype(BF)
    e2 = np.zeros((2, 128), np.float32)
    e2[0, 0:64] = 1.0
    e2[1, 64:128] = 1.0
    g["c_e2"] = e2
    eh = np.zeros((16, 128), np.float32)
    for h in range(16):
        eh[h, (h % 2) * 64:(h % 2) * 64 + 64] = 1.0
    g["c_ehalf"] = eh
    s168 = np.zeros((16, 8), np.float32)
    for h in range(16):
        s168[h, h // 2] = 1.0
    g["c_sel168"] = s168
    g["c_ecol"] = np.arange(8, dtype=np.float32).reshape(8, 1) * C_CAP - 1.0
    return g


def make_in_maps(inputs):
    g = _prep_shared(inputs)
    x = inputs["L0_attn_X"]
    in_maps = []
    for c in range(NCORE):
        m = dict(g)
        m["x0_fm"] = np.ascontiguousarray(x[2 * c:2 * c + 2].reshape(T, D).T)
        m["bisrows"] = np.array([(r * 8 + c) * 4 + q for r in range(8) for q in range(4)], dtype=np.int32).reshape(32, 1)
        in_maps.append(m)
    return in_maps


def kernel(**inputs):
    in_maps = make_in_maps(inputs)
    nc = build_program()
    res = run_bass_kernel_spmd(nc, in_maps, core_ids=list(range(NCORE)))
    y = np.concatenate([res.results[c]["y"] for c in range(NCORE)], axis=0)
    return y.reshape(16, 2048, D)
